# version 02
# baseline (speedup 1.0000x reference)
import numpy as np
from contextlib import ExitStack

import concourse.bass as bass
import concourse.mybir as mybir
from concourse.bass_utils import run_bass_kernel_spmd

F32 = mybir.dt.float32
BF16 = mybir.dt.bfloat16
I32 = mybir.dt.int32
AF = mybir.ActivationFunctionType
OP = mybir.AluOpType
AX = mybir.AxisListType

B, T, D = 4, 4096, 1024
TH = T // 2
NCORES = 8

ENGINES = ("tensor", "vector", "scalar", "gpsimd", "sync")
DMAQ = {"dg": "gpsimd", "dh": "sync", "da": "scalar"}


class Prog:

    def __init__(self, nc, es):
        self.nc = nc
        self.sems = {}
        self.step = {}
        for e in ENGINES:
            self.sems[e] = es.enter_context(nc.semaphore("s_" + e))
            self.step[e] = 1
        for q in DMAQ:
            self.sems[q] = es.enter_context(nc.semaphore("s_" + q))
            self.step[q] = 16
        self.count = {k: 0 for k in self.sems}
        self.ops = {k: [] for k in ENGINES}
        self.last_w = {}
        self.readers = {}
        self.waited = {}

    def _emit(self, engine, stream, fn, reads, writes, extra_deps):
        deps = {}
        def add(h):
            if h is None:
                return
            s, v = h
            if deps.get(s, 0) < v:
                deps[s] = v
        for r in reads:
            add(self.last_w.get(r))
        for w in writes:
            add(self.last_w.get(w))
            for h in self.readers.get(w, ()):
                add(h)
        for h in extra_deps or ():
            add(h)
        for s in list(deps):
            if s in DMAQ and deps[s] > 0:
                deps[s] = self.count[s]
        if stream in DMAQ:
            if self.count[stream] > 0:
                deps[stream] = self.count[stream]
        self.count[stream] += self.step[stream]
        h = (stream, self.count[stream])
        waits = []
        for s, v in deps.items():
            if s == stream == "tensor" and engine == "tensor":
                continue
            key = (engine, s)
            if self.waited.get(key, 0) >= v:
                continue
            waits.append((s, v))
            self.waited[key] = v
        self.ops[engine].append((waits, fn, stream, self.step[stream]))
        for r in reads:
            self.readers.setdefault(r, []).append(h)
        for w in writes:
            self.last_w[w] = h
            self.readers[w] = []
        return h

    def op(self, engine, fn, reads=(), writes=(), deps=None):
        return self._emit(engine, engine, fn, reads, writes, deps)

    def dma(self, queue, fn, reads=(), writes=(), deps=None):
        return self._emit(DMAQ[queue], queue, fn, reads, writes, deps)

    def finish(self, blk):
        for e in ENGINES:
            entries = self.ops[e]
            if not entries:
                continue
            def make(entries, sems=self.sems):
                def body(eng):
                    for waits, fn, stream, step in entries:
                        for s, v in waits:
                            eng.wait_ge(sems[s], v)
                        fn(eng).then_inc(sems[stream], step)
                return body
            getattr(blk, e)(make(list(entries)))
        fin = [(s, v) for s, v in self.count.items() if v > 0]
        def barrier(eng):
            for s, v in fin:
                eng.wait_ge(self.sems[s], v)
        blk.scalar(barrier)


class Ctx:
    def __init__(self, name):
        self.name = name
        self.nc = bass.Bass("TRN2", target_bir_lowering=False)
        self.es = ExitStack()
        self.p = Prog(self.nc, self.es)
        self.blk = None

    def sb(self, name, shape, dt):
        return self.es.enter_context(self.nc.sbuf_tensor(name, shape, dt))

    def ps(self, name, shape, dt=F32):
        return self.es.enter_context(self.nc.psum_tensor(name, shape, dt))

    def din(self, name, shape, dt=F32):
        return self.nc.dram_tensor(name, shape, dt, kind="ExternalInput").ap()

    def dout(self, name, shape, dt=F32):
        return self.nc.dram_tensor(name, shape, dt, kind="ExternalOutput").ap()

    def begin(self):
        self.blk = self.es.enter_context(self.nc.Block())

    def cap(self, val):
        if not hasattr(self, "cbuf"):
            self.cbuf = self.sb("cbuf", [128, 32], F32)
            self.cmap = {}
        if val not in self.cmap:
            j = len(self.cmap)
            assert j < 32
            self.cmap[val] = j
            self.p.op("vector", lambda v, j=j, val=val: v.memset(
                self.cbuf[:, j:j + 1], val), writes=["__c"])
        j = self.cmap[val]
        return self.cbuf[:, j:j + 1]

    def done(self):
        self.p.finish(self.blk)
        self.es.close()
        return self.nc


def bcast_inner(ap2d, k):
    return bass.AP(ap2d.tensor, ap2d.offset,
                   [list(x) for x in ap2d.ap] + [[0, k]])


def bcast_dram(ap2d, np_=128):
    return bass.AP(ap2d.tensor, ap2d.offset, [[0, np_]] + [list(x) for x in ap2d.ap])


def load_ones(ct):
    p = ct.p
    ones1f = ct.sb("ones1f", [1, 128], F32)
    p.op("vector", lambda v: v.memset(ones1f[:, :], 1.0), writes=["ones1f"])
    return ones1f



def build_A0(dbg=False):
    ct = Ctx("A0")
    p = ct.p
    xT = ct.din("xT", [D, T])
    wqI = ct.din("wqI", [D, 512])
    wkI = ct.din("wkI", [D, 512])
    wvI = ct.din("wvI", [D, 512])
    ropeC = ct.din("ropeC", [128, T])
    ropeS = ct.din("ropeS", [128, T])
    permI = ct.din("permI", [128, 128])
    dmaskI = ct.din("dmask", [128, 2048])
    AO = ct.dout("AO", [512, T])
    if dbg:
        DQ = ct.dout("DQ", [512, T], BF16)
        DK = ct.dout("DK", [512, T], BF16)
        DV = ct.dout("DV", [128, 32 * 520], BF16)

    NP, TQ, KT = 4, 512, 128
    NT = T // TQ
    NKT = T // KT

    xc = [ct.sb(f"xc{i}", [128, TQ], BF16) for i in range(8)]
    wq = ct.sb("wq", [128, 8 * 512], BF16)
    wk = ct.sb("wk", [128, 8 * 512], BF16)
    wv = ct.sb("wv", [128, 8 * 512], BF16)
    cs = ct.sb("cs", [128, T], BF16)
    sg = ct.sb("sg", [128, T], BF16)
    pm = ct.sb("pm", [128, 128], BF16)
    mk = ct.sb("mk", [128, 4 * 512], BF16)
    q16 = [ct.sb(f"q16_{i}", [128, T], BF16) for i in range(NP)]
    k16 = [ct.sb(f"k16_{i}", [128, T], BF16) for i in range(NP)]
    qp16 = ct.sb("qp16", [128, TQ], BF16)
    r1 = ct.sb("r1", [128, TQ], F32)
    r2 = ct.sb("r2", [128, TQ], F32)
    vt = ct.sb("vt", [128, NKT * 520], BF16)
    pexp = ct.sb("pexp", [128, TQ], BF16)
    ao_sb = ct.sb("ao_sb", [64, TQ], F32)
    dn = ct.sb("dn", [1, TQ], F32)
    rdn = ct.sb("rdn", [1, TQ], F32)
    dbc = ct.sb("dbc", [64, TQ], F32)

    ps_p = ct.ps("ps_p", [128, 512])
    ps_w = ct.ps("ps_w", [128, 512])
    ps_s = ct.ps("ps_s", [128, 512])
    ps_o = ct.ps("ps_o", [128, 512])
    ps_v = ct.ps("ps_v", [128, 512])
    ps_b = ct.ps("ps_b", [64, 512])

    ct.begin()
    ones1f = load_ones(ct)

    p.dma("dg", lambda g: g.dma_start(out=cs[:, :], in_=ropeC), writes=["cs"])
    p.dma("dg", lambda g: g.dma_start(out=sg[:, :], in_=ropeS), writes=["sg"])
    p.dma("dg", lambda g: g.dma_start(out=pm[:, :], in_=permI), writes=["pm"])
    p.dma("dg", lambda g: g.dma_start(out=mk[:, :], in_=dmaskI), writes=["mk"])
    for i in range(8):
        p.dma("dg", lambda g, i=i: g.dma_start(
            out=wq[:, i * 512:(i + 1) * 512], in_=wqI[i * 128:(i + 1) * 128, :]),
            writes=["wq"])
        p.dma("dg", lambda g, i=i: g.dma_start(
            out=wk[:, i * 512:(i + 1) * 512], in_=wkI[i * 128:(i + 1) * 128, :]),
            writes=["wk"])
        p.dma("dg", lambda g, i=i: g.dma_start(
            out=wv[:, i * 512:(i + 1) * 512], in_=wvI[i * 128:(i + 1) * 128, :]),
            writes=["wv"])
    for h in range(8):
        p.op("vector", lambda v, h=h: v.memset(
            vt.rearrange("p (k c) -> p k c", c=520)[:, :, h * 65 + 64], 1.0),
            writes=[f"vt_ones{h}"])

    for tc in range(NT):
        sl = slice(tc * TQ, (tc + 1) * TQ)
        for i in range(8):
            p.dma("dg", lambda g, i=i, tc=tc: g.dma_start(
                out=xc[i][:, :], in_=xT[i * 128:(i + 1) * 128,
                                        tc * TQ:(tc + 1) * TQ]),
                writes=[f"xc{i}"])
        for pt in range(NP):
            for wsb, wname, dst, dname in ((wq, "wq", q16, "q16"),
                                           (wk, "wk", k16, "k16")):
                for i in range(8):
                    p.op("tensor", lambda t, i=i, pt=pt, wsb=wsb: t.matmul(
                        ps_p[:, :], wsb[:, i * 512 + pt * 128:
                                        i * 512 + pt * 128 + 128],
                        xc[i][:, :], start=(i == 0), stop=(i == 7)),
                        reads=[wname, f"xc{i}"], writes=["ps_p"])
                p.op("vector", lambda v: v.tensor_copy(qp16[:, :], ps_p[:, :]),
                     reads=["ps_p"], writes=["qp16"])
                p.op("tensor", lambda t: t.matmul(
                    ps_w[:, :], pm[:, :], qp16[:, :], start=True, stop=True),
                    reads=["pm", "qp16"], writes=["ps_w"])
                p.op("vector", lambda v, sl=sl: v.tensor_mul(
                    r1[:, :], qp16[:, :], cs[:, sl]),
                    reads=["qp16", "cs"], writes=["r1"])
                p.op("vector", lambda v, sl=sl: v.tensor_mul(
                    r2[:, :], ps_w[:, :], sg[:, sl]),
                    reads=["ps_w", "sg"], writes=["r2"])
                p.op("vector", lambda v, pt=pt, sl=sl, dst=dst, dname=dname:
                     v.tensor_add(dst[pt][:, sl], r1[:, :], r2[:, :]),
                     reads=["r1", "r2"], writes=[f"{dname}{pt}"])
        for sub in range(TQ // KT):
            kt = tc * 4 + sub
            for i in range(8):
                p.op("tensor", lambda t, i=i, sub=sub: t.matmul(
                    ps_v[:, :], xc[i][:, sub * KT:(sub + 1) * KT],
                    wv[:, i * 512:(i + 1) * 512],
                    start=(i == 0), stop=(i == 7)),
                    reads=[f"xc{i}", "wv"], writes=["ps_v"])
            p.op("vector", lambda v, kt=kt: v.tensor_copy(
                vt.rearrange("p (k c) -> p k c", c=520)[:, kt, 0:520]
                  .rearrange("p (h c) -> p h c", c=65)[:, :, 0:64],
                ps_v.rearrange("p (h c) -> p h c", c=64)[:, :, :]),
                reads=["ps_v"], writes=[f"vt_{kt}"])

    if dbg:
        for ptq in range(NP):
            p.dma("dh", lambda s, ptq=ptq: s.dma_start(
                out=DQ[ptq * 128:(ptq + 1) * 128, :], in_=q16[ptq][:, :]),
                reads=[f"q16{ptq}"], writes=[f"DQ{ptq}"])
            p.dma("dh", lambda s, ptq=ptq: s.dma_start(
                out=DK[ptq * 128:(ptq + 1) * 128, :], in_=k16[ptq][:, :]),
                reads=[f"k16{ptq}"], writes=[f"DK{ptq}"])
        p.dma("dh", lambda s: s.dma_start(out=DV, in_=vt[:, :]),
              reads=[f"vt_{k}" for k in range(32)] + [f"vt_ones{h}" for h in range(8)],
              writes=["DVo"])

    for tc in range(NT):
        qsl = slice(tc * TQ, (tc + 1) * TQ)
        nkt = 4 * tc + 4
        for h in range(8):
            pt, hh = h // 2, h % 2
            for kt in range(nkt):
                ksl = slice(kt * KT, (kt + 1) * KT)
                p.op("tensor", lambda t, pt=pt, hh=hh, ksl=ksl, qsl=qsl:
                     t.matmul(ps_s[:, :], k16[pt][hh * 64:hh * 64 + 64, ksl],
                              q16[pt][hh * 64:hh * 64 + 64, qsl],
                              start=True, stop=True),
                     reads=[f"k16{pt}", f"q16{pt}"], writes=["ps_s"])
                b12 = ct.cap(-12.0)
                p.op("scalar", lambda sc, b12=b12: sc.activation(
                    pexp[:, :], ps_s[:, :], AF.Exp, bias=b12),
                    reads=["ps_s", "__c"], writes=["pexp"])
                rel = kt - tc * 4
                if 0 <= rel < 4:
                    p.op("vector", lambda v, rel=rel: v.tensor_mul(
                        pexp[:, :], pexp[:, :], mk[:, rel * 512:(rel + 1) * 512]),
                        reads=["pexp", "mk"], writes=["pexp"])
                p.op("tensor", lambda t, kt=kt, h=h, nkt=nkt: t.matmul(
                    ps_o[0:65, :],
                    vt.rearrange("p (k c) -> p k c", c=520)[:, kt,
                                                            h * 65:h * 65 + 65],
                    pexp[:, :], start=(kt == 0), stop=(kt == nkt - 1)),
                    reads=[f"vt_{kt}", f"vt_ones{h}", "pexp"], writes=["ps_o"])
            p.op("vector", lambda v: v.tensor_copy(dn[:, :], ps_o[64:65, :]),
                 reads=["ps_o"], writes=["dn"])
            p.op("vector", lambda v: v.reciprocal(rdn[:, :], dn[:, :]),
                 reads=["dn"], writes=["rdn"])
            p.op("tensor", lambda t: t.matmul(
                ps_b[:, :], ones1f[0:1, 0:64], rdn[:, :], start=True, stop=True),
                reads=["rdn", "ones1f"], writes=["ps_b"])
            p.op("scalar", lambda sc: sc.copy(dbc[:, :], ps_b[:, :]),
                 reads=["ps_b"], writes=["dbc"])
            p.op("vector", lambda v: v.tensor_mul(
                ao_sb[:, :], ps_o[0:64, :], dbc[:, :]),
                reads=["ps_o", "dbc"], writes=["ao_sb"])
            p.dma("dh", lambda s, h=h, qsl=qsl: s.dma_start(
                out=AO[h * 64:(h + 1) * 64, qsl], in_=ao_sb[:, :]),
                reads=["ao_sb"], writes=[f"AO_{tc}_{h}"])

    return ct.done()



def rope_permute_cols(w):
    cols = []
    for i in range(4):
        for hh in range(2):
            h = 2 * i + hh
            cols.append(np.arange(h * 64 + 0, h * 64 + 64, 2))
            cols.append(np.arange(h * 64 + 1, h * 64 + 64, 2))
    order = np.concatenate(cols)
    return np.ascontiguousarray(w[order, :].T)


def rope_tables():
    hd = 64
    theta = np.exp(np.log(10000.0) * np.arange(hd // 2, dtype=np.float64) / hd)
    args = np.arange(T, dtype=np.float64)[None, :] * theta[:, None]
    sc = hd ** (-0.25)
    c = (np.cos(args) * sc).astype(np.float32)
    s = (np.sin(args) * sc).astype(np.float32)
    C = np.concatenate([c, c, c, c], 0)
    S = np.concatenate([-s, s, -s, s], 0)
    return C, S


def rope_perm():
    pmat = np.zeros((128, 128), np.float32)
    for blk in range(4):
        src = [32, 0, 96, 64][blk]
        for r in range(32):
            pmat[src + r, blk * 32 + r] = 1.0
    return pmat


def diag_masks():
    m = np.zeros((4, 128, 512), np.float32)
    for r in range(4):
        tk = np.arange(128)[:, None]
        tq = np.arange(512)[None, :]
        m[r] = (tq >= tk + 128 * r).astype(np.float32)
    return np.ascontiguousarray(m.transpose(1, 0, 2).reshape(128, 2048))


def a0_inputs(x, wq, wk, wv):
    C, S = rope_tables()
    dm = diag_masks()
    pmat = rope_perm()
    maps = []
    for c in range(NCORES):
        b, hg = c // 2, c % 2
        hsl = slice(hg * 512, (hg + 1) * 512)
        maps.append({
            "xT": np.ascontiguousarray(x[b].T),
            "wqI": rope_permute_cols(wq[hsl]),
            "wkI": rope_permute_cols(wk[hsl]),
            "wvI": np.ascontiguousarray(wv[hsl].T),
            "ropeC": C, "ropeS": S, "permI": pmat, "dmask": dm,
        })
    return maps




def make_ln_scratch(ct):
    s = {}
    s["sq"] = ct.sb("lnsq", [128, 512], BF16)
    s["xc"] = ct.sb("lnxc", [128, 512], BF16)
    s["st"] = ct.sb("lnst", [1, 4 * 512], F32)
    s["ones128b"] = ct.sb("ones128b", [128, 1], BF16)
    ct.p.op("vector", lambda v: v.memset(s["ones128b"][:, :], 1.0),
            writes=["ones128b"])
    s["ps1"] = ct.ps("lnps1", [1, 512])
    s["ps2"] = ct.ps("lnps2", [1, 512])
    s["psa"] = ct.ps("lnpsa", [128, 512])
    s["psb"] = ct.ps("lnpsb", [128, 512])
    return s


def emit_ln(ct, xf, xkeys, s, W, xb16=None, xb16keys=None):
    p = ct.p
    for c in range(W // 512):
        sl = slice(c * 512, (c + 1) * 512)
        for i in range(8):
            p.op("vector", lambda v, i=i, sl=sl: v.tensor_copy(
                s["xc"][:, :], xf[i][:, sl]),
                reads=[xkeys[i]], writes=["lnxc"])
            p.op("scalar", lambda sc, i=i, sl=sl: sc.activation(
                s["sq"][:, :], xf[i][:, sl], AF.Square),
                reads=[xkeys[i]], writes=["lnsq"])
            p.op("tensor", lambda t, i=i: t.matmul(
                s["ps1"][:, :], s["ones128b"][:, :], s["xc"][:, :],
                start=(i == 0), stop=(i == 7)),
                reads=["ones128b", "lnxc"], writes=["lnps1"])
            p.op("tensor", lambda t, i=i: t.matmul(
                s["ps2"][:, :], s["ones128b"][:, :], s["sq"][:, :],
                start=(i == 0), stop=(i == 7)),
                reads=["ones128b", "lnsq"], writes=["lnps2"])
        st = s["st"]
        p.op("scalar", lambda sc: sc.activation(
            st[:, 0:512], s["ps1"][:, :], AF.Copy, scale=1.0 / 1024),
            reads=["lnps1"], writes=["ln_mean"])
        p.op("scalar", lambda sc: sc.activation(
            st[:, 512:1024], s["ps2"][:, :], AF.Copy, scale=1.0 / 1024),
            reads=["lnps2"], writes=["ln_var"])
        p.op("scalar", lambda sc: sc.activation(
            st[:, 1024:1536], st[:, 0:512], AF.Square),
            reads=["ln_mean"], writes=["ln_m2"])
        p.op("vector", lambda v: v.tensor_sub(
            st[:, 512:1024], st[:, 512:1024], st[:, 1024:1536]),
            reads=["ln_var", "ln_m2"], writes=["ln_var"])
        eps = ct.cap(1e-5)
        p.op("scalar", lambda sc, eps=eps: sc.activation(
            st[:, 1024:1536], st[:, 512:1024], AF.Sqrt, bias=eps[0:1, :]),
            reads=["ln_var", "__c"], writes=["ln_sd"])
        p.op("vector", lambda v: v.reciprocal(
            st[:, 1024:1536], st[:, 1024:1536]),
            reads=["ln_sd"], writes=["ln_rstd"])
        p.op("vector", lambda v: v.tensor_mul(
            st[:, 1536:2048], st[:, 0:512], st[:, 1024:1536]),
            reads=["ln_mean", "ln_rstd"], writes=["ln_mrs"])
        p.op("vector", lambda v: v.tensor_scalar_mul(
            st[:, 1536:2048], st[:, 1536:2048], -1.0),
            reads=["ln_mrs"], writes=["ln_mrs"])
        p.op("tensor", lambda t: t.matmul(
            s["psa"][:, :], ct._ones1f[0:1, :], st[:, 1024:1536],
            start=True, stop=True),
            reads=["ln_rstd", "ones1f"], writes=["lnpsa"])
        p.op("tensor", lambda t: t.matmul(
            s["psb"][:, :], ct._ones1f[0:1, :], st[:, 1536:2048],
            start=True, stop=True),
            reads=["ln_mrs", "ones1f"], writes=["lnpsb"])
        for i in range(8):
            p.op("vector", lambda v, i=i, sl=sl: v.tensor_mul(
                xf[i][:, sl], xf[i][:, sl], s["psa"][:, :]),
                reads=[xkeys[i], "lnpsa"], writes=[xkeys[i]])
            p.op("vector", lambda v, i=i, sl=sl: v.tensor_add(
                xf[i][:, sl], xf[i][:, sl], s["psb"][:, :]),
                reads=[xkeys[i], "lnpsb"], writes=[xkeys[i]])
            if xb16 is not None:
                p.op("vector", lambda v, i=i, sl=sl: v.tensor_copy(
                    xb16[i][:, sl], xf[i][:, sl]),
                    reads=[xkeys[i]], writes=[xb16keys[i]])



def build_A1(dbg=False):
    ct = Ctx("A1")
    p = ct.p
    xTh = ct.din("xTh", [D, TH])
    AOh = ct.din("AOh", [D, TH])
    woTI = ct.din("woT", [D, D])
    WgI = ct.din("Wg", [D, 8])
    W1I = ct.din("W1", [8, D, 2048])
    W2I = ct.din("W2", [8, 2048, D])
    identI = ct.din("identI", [128, 128])
    selI = ct.din("selI", [8, 1024])
    X2 = ct.dout("X2", [D, TH])
    if dbg:
        CMB = ct.dout("CMB", [8, TH])
        X1D = ct.dout("X1D", [D, TH])

    xf = [ct.sb(f"xf{i}", [128, TH], F32) for i in range(8)]
    xb16 = [ct.sb(f"xb{i}", [128, TH], BF16) for i in range(8)]
    ao16 = ct.sb("ao16", [128, 512], BF16)
    w1e = ct.sb("w1e", [128, 8 * 2048], BF16)
    w2e = ct.sb("w2e", [128, 16 * 1024], BF16)
    h16 = ct.sb("h16", [128, 16 * 512], BF16)
    hrl = ct.sb("hrl", [128, 512], F32)
    srt = ct.sb("srt", [128, 512], F32)
    wg16 = ct.sb("wg16", [128, 64], F32)
    lg = ct.sb("lg", [128, 8], F32)
    e8 = ct.sb("e8", [128, 8], F32)
    pm8 = ct.sb("pm8", [128, 8], F32)
    eq1 = ct.sb("eq1", [128, 8], F32)
    eq2 = ct.sb("eq2", [128, 8], F32)
    z8 = ct.sb("z8", [128, 8], F32)
    ctm = ct.sb("ctm", [128, 8], F32)
    cfm = ct.sb("cfm", [8, TH], F32)
    sel = ct.sb("sel", [8, 1024], F32)
    c1 = ct.sb("c1", [128, 1], F32)
    c2 = ct.sb("c2", [128, 1], F32)
    c3 = ct.sb("c3", [128, 1], F32)
    idb = ct.sb("idb", [128, 128], F32)

    ps_m = ct.ps("ps_m", [128, 512])
    ps_c = ct.ps("ps_c", [128, 512])
    ps_l = ct.ps("ps_l", [128, 512])
    ps_t = ct.ps("ps_t", [128, 128])

    ct.begin()
    ones1f = load_ones(ct)
    ct._ones1f = ones1f
    lns = make_ln_scratch(ct)
    p.dma("dg", lambda g: g.dma_start(out=idb[:, :], in_=identI),
          writes=["idb"])
    p.dma("dg", lambda g: g.dma_start(out=sel[:, :], in_=selI), writes=["sel"])
    p.op("vector", lambda v: v.memset(z8[:, :], 0.0), writes=["z8"])
    p.dma("dg", lambda g: g.dma_start(
        out=wg16.rearrange("p (i c) -> p i c", c=8),
        in_=WgI.rearrange("(i p) c -> p i c", p=128)),
        writes=["wg16"])
    for i in range(8):
        p.dma("dg", lambda g, i=i: g.dma_start(
            out=xf[i][:, :], in_=xTh[i * 128:(i + 1) * 128, :]),
            writes=[f"xf{i}"])

    for do in range(8):
        p.dma("dg", lambda g, do=do: g.dma_start(
            out=w2e[:, 0:1024].rearrange("p (k c) -> p k c", c=128),
            in_=woTI[:, do * 128:(do + 1) * 128]
                .rearrange("(k p) c -> p k c", p=128)),
            writes=["w2e"])
        for tch in range(TH // 512):
            sl = slice(tch * 512, (tch + 1) * 512)
            for kt in range(8):
                p.dma("dg", lambda g, kt=kt, sl=sl: g.dma_start(
                    out=ao16[:, :], in_=AOh[kt * 128:(kt + 1) * 128, sl]),
                    writes=["ao16"])
                p.op("tensor", lambda t, kt=kt: t.matmul(
                    ps_m[:, :], w2e[:, kt * 128:(kt + 1) * 128], ao16[:, :],
                    start=(kt == 0), stop=(kt == 7)),
                    reads=["w2e", "ao16"], writes=["ps_m"])
            p.op("vector", lambda v, do=do, sl=sl: v.tensor_add(
                xf[do][:, sl], xf[do][:, sl], ps_m[:, :]),
                reads=[f"xf{do}", "ps_m"], writes=[f"xf{do}"])

    xkeys = [f"xf{i}" for i in range(8)]
    xbkeys = [f"xb{i}" for i in range(8)]
    emit_ln(ct, xf, xkeys, lns, TH, xb16, xbkeys)
    if dbg:
        for i in range(8):
            p.dma("dh", lambda s, i=i: s.dma_start(
                out=X1D[i * 128:(i + 1) * 128, :], in_=xf[i][:, :]),
                reads=[f"xf{i}"], writes=[f"X1D{i}"])

    for tt in range(TH // 128):
        tsl = slice(tt * 128, (tt + 1) * 128)
        for i in range(8):
            p.op("tensor", lambda t, i=i, tsl=tsl: t.matmul(
                ps_l[:, 0:8], xf[i][:, tsl], wg16[:, i * 8:(i + 1) * 8],
                start=(i == 0), stop=(i == 7)),
                reads=[f"xf{i}", "wg16"], writes=["ps_l"])
        p.op("vector", lambda v: v.tensor_copy(lg[:, :], ps_l[:, 0:8]),
             reads=["ps_l"], writes=["lg"])
        p.op("vector", lambda v: v.reduce_max(c1[:, :], lg[:, :], axis=AX.X),
             reads=["lg"], writes=["c1"])
        p.op("vector", lambda v: v.tensor_scalar_mul(c1[:, :], c1[:, :], -1.0),
             reads=["c1"], writes=["c1"])
        p.op("scalar", lambda sc: sc.activation(
            e8[:, :], lg[:, :], AF.Exp, bias=c1[:, 0:1], accum_out=c2[:, 0:1]),
            reads=["lg", "c1"], writes=["e8", "c2"])
        p.op("vector", lambda v: v.reciprocal(c3[:, :], c2[:, :]),
             reads=["c2"], writes=["c3"])
        p.op("vector", lambda v: v.tensor_mul(
            e8[:, :], e8[:, :], c3[:, 0:1].broadcast_to([128, 8])),
            reads=["e8", "c3"], writes=["e8"])
        p.op("vector", lambda v: v.reduce_max(c1[:, :], e8[:, :], axis=AX.X),
             reads=["e8"], writes=["c1"])
        p.op("vector", lambda v: v.tensor_tensor(
            eq1[:, :], e8[:, :], c1[:, 0:1].broadcast_to([128, 8]), OP.is_ge),
            reads=["e8", "c1"], writes=["eq1"])
        p.op("vector", lambda v: v.scalar_tensor_tensor(
            pm8[:, :], eq1[:, :], -2.0, e8[:, :], OP.mult, OP.add),
            reads=["eq1", "e8"], writes=["pm8"])
        p.op("vector", lambda v: v.reduce_max(c2[:, :], pm8[:, :], axis=AX.X),
             reads=["pm8"], writes=["c2"])
        p.op("vector", lambda v: v.tensor_tensor(
            eq2[:, :], pm8[:, :], c2[:, 0:1].broadcast_to([128, 8]), OP.is_ge),
            reads=["pm8", "c2"], writes=["eq2"])
        p.op("vector", lambda v: v.tensor_add(c3[:, :], c1[:, :], c2[:, :]),
             reads=["c1", "c2"], writes=["c3"])
        p.op("vector", lambda v: v.tensor_scalar_add(c3[:, :], c3[:, :], 1e-8),
             reads=["c3"], writes=["c3"])
        p.op("vector", lambda v: v.reciprocal(c3[:, :], c3[:, :]),
             reads=["c3"], writes=["c3"])
        p.op("vector", lambda v: v.tensor_mul(c1[:, :], c1[:, :], c3[:, :]),
             reads=["c1", "c3"], writes=["c1"])
        p.op("vector", lambda v: v.tensor_mul(c2[:, :], c2[:, :], c3[:, :]),
             reads=["c2", "c3"], writes=["c2"])
        p.op("vector", lambda v: v.scalar_tensor_tensor(
            pm8[:, :], eq1[:, :], c1[:, 0:1], z8[:, :], OP.mult, OP.add),
            reads=["eq1", "c1", "z8"], writes=["pm8"])
        p.op("vector", lambda v: v.scalar_tensor_tensor(
            pm8[:, :], eq2[:, :], c2[:, 0:1], pm8[:, :], OP.mult, OP.add),
            reads=["eq2", "c2", "pm8"], writes=["pm8"])
        p.op("vector", lambda v: v.tensor_copy(ctm[:, :], pm8[:, :]),
             reads=["pm8"], writes=["ctm"])
        p.op("tensor", lambda t: t.transpose(
            ps_t[0:8, 0:128], ctm[:, :], idb[:, :]),
            reads=["ctm", "idb"], writes=["ps_t"])
        p.op("vector", lambda v, tsl=tsl: v.tensor_copy(
            cfm[:, tsl], ps_t[0:8, 0:128]),
            reads=["ps_t"], writes=["cfm"])
    if dbg:
        p.dma("dh", lambda s: s.dma_start(out=CMB, in_=cfm[:, :]),
              reads=["cfm"], writes=["CMBo"])

    for e in range(8):
        p.dma("dg", lambda g, e=e: g.dma_start(
            out=w1e.rearrange("p (i c) -> p i c", c=2048),
            in_=W1I[e].rearrange("(i p) c -> p i c", p=128)),
            writes=["w1e"])
        p.dma("dg", lambda g, e=e: g.dma_start(
            out=w2e.rearrange("p (i c) -> p i c", c=1024),
            in_=W2I[e].rearrange("(i p) c -> p i c", p=128)),
            writes=["w2e"])
        for tch in range(TH // 512):
            sl = slice(tch * 512, (tch + 1) * 512)
            p.op("tensor", lambda t, e=e, sl=sl: t.matmul(
                ps_c[:, :], sel[:, e * 128:(e + 1) * 128], cfm[:, sl],
                start=True, stop=True),
                reads=["cfm", "sel"], writes=["ps_c"])
            p.op("scalar", lambda sc: sc.activation(
                srt[:, :], ps_c[:, :], AF.Sqrt),
                reads=["ps_c"], writes=["srt"])
            for ht in range(16):
                for i in range(8):
                    p.op("tensor", lambda t, i=i, ht=ht, sl=sl: t.matmul(
                        ps_m[:, :], w1e[:, i * 2048 + ht * 128:
                                        i * 2048 + ht * 128 + 128],
                        xb16[i][:, sl], start=(i == 0), stop=(i == 7)),
                        reads=["w1e", f"xb{i}"], writes=["ps_m"])
                p.op("scalar", lambda sc: sc.activation(
                    hrl[:, :], ps_m[:, :], AF.Relu),
                    reads=["ps_m"], writes=["hrl"])
                p.op("vector", lambda v: v.tensor_mul(
                    hrl[:, :], hrl[:, :], srt[:, :]),
                    reads=["hrl", "srt"], writes=["hrl"])
                p.op("vector", lambda v, ht=ht: v.tensor_mul(
                    h16[:, ht * 512:(ht + 1) * 512], hrl[:, :], hrl[:, :]),
                    reads=["hrl"], writes=["h16"])
            for do in range(8):
                for ht in range(16):
                    p.op("tensor", lambda t, do=do, ht=ht: t.matmul(
                        ps_m[:, :], w2e[:, ht * 1024 + do * 128:
                                        ht * 1024 + do * 128 + 128],
                        h16[:, ht * 512:(ht + 1) * 512],
                        start=(ht == 0), stop=(ht == 15)),
                        reads=["w2e", "h16"], writes=["ps_m"])
                p.op("vector", lambda v, do=do, sl=sl: v.tensor_add(
                    xf[do][:, sl], xf[do][:, sl], ps_m[:, :]),
                    reads=[f"xf{do}", "ps_m"], writes=[f"xf{do}"])

    emit_ln(ct, xf, xkeys, lns, TH)
    for i in range(8):
        p.dma("dh", lambda s, i=i: s.dma_start(
            out=X2[i * 128:(i + 1) * 128, :], in_=xf[i][:, :]),
            reads=[f"xf{i}"], writes=[f"X2_{i}"])

    return ct.done()


def selblk():
    s = np.zeros((8, 1024), np.float32)
    for e in range(8):
        s[e, e * 128:(e + 1) * 128] = 1.0
    return s


def a1_inputs(x, ao, wo, Wg, W1, W2):
    ident = np.eye(128, dtype=np.float32)
    maps = []
    for c in range(NCORES):
        b, j = c // 2, c % 2
        sl = slice(j * TH, (j + 1) * TH)
        maps.append({
            "xTh": np.ascontiguousarray(x[b].T[:, sl]),
            "AOh": np.ascontiguousarray(ao[b].T[:, sl]),
            "woT": np.ascontiguousarray(wo.T),
            "Wg": Wg, "W1": W1, "W2": W2, "identI": ident,
            "selI": selblk(),
        })
    return maps




def build_A2():
    ct = Ctx("A2")
    p = ct.p
    xT = ct.din("xT", [D, T])
    wkTI = ct.din("wkT", [D, 512])
    bkI = ct.din("bk", [512, 1])
    identI = ct.din("identI", [128, 128])
    AO1 = ct.dout("AO1", [512, T])

    NP, TQ, KT = 4, 512, 128
    NT = T // TQ
    NKT = T // KT

    xc = [ct.sb(f"xc{i}", [128, TQ], BF16) for i in range(8)]
    wk = ct.sb("wk", [128, 8 * 512], BF16)
    bk = [ct.sb(f"bk{i}", [128, 1], F32) for i in range(NP)]
    k16 = [ct.sb(f"k16_{i}", [128, T], BF16) for i in range(NP)]
    idb = ct.sb("idb", [128, 128], BF16)
    vt = ct.sb("vt", [128, NKT * 520], BF16)
    pexp = ct.sb("pexp", [128, TQ], BF16)
    ao_sb = ct.sb("ao_sb", [64, TQ], F32)
    dn = ct.sb("dn", [1, TQ], F32)
    rdn = ct.sb("rdn", [1, TQ], F32)
    dbc = ct.sb("dbc", [64, TQ], F32)

    ps_p = ct.ps("ps_p", [128, 512])
    ps_s = ct.ps("ps_s", [128, 512])
    ps_o = ct.ps("ps_o", [128, 512])
    ps_t = ct.ps("ps_t", [128, 128], BF16)
    ps_b = ct.ps("ps_b", [64, 512])

    ct.begin()
    ones1f = load_ones(ct)
    ct._ones1f = ones1f

    p.dma("dg", lambda g: g.dma_start(out=idb[:, :], in_=identI),
          writes=["idb"])
    for i in range(8):
        p.dma("dg", lambda g, i=i: g.dma_start(
            out=wk[:, i * 512:(i + 1) * 512], in_=wkTI[i * 128:(i + 1) * 128, :]),
            writes=["wk"])
    for ptt in range(NP):
        p.dma("dg", lambda g, ptt=ptt: g.dma_start(
            out=bk[ptt][:, :], in_=bkI[ptt * 128:(ptt + 1) * 128, :]),
            writes=[f"bk{ptt}"])
    for h in range(8):
        p.op("vector", lambda v, h=h: v.memset(
            vt.rearrange("p (k c) -> p k c", c=520)[:, :, h * 65 + 64], 1.0),
            writes=[f"vt_ones{h}"])

    for tc in range(NT):
        sl = slice(tc * TQ, (tc + 1) * TQ)
        for i in range(8):
            p.dma("dg", lambda g, i=i, tc=tc: g.dma_start(
                out=xc[i][:, :], in_=xT[i * 128:(i + 1) * 128,
                                        tc * TQ:(tc + 1) * TQ]),
                writes=[f"xc{i}"])
        for ptt in range(NP):
            for i in range(8):
                p.op("tensor", lambda t, i=i, ptt=ptt: t.matmul(
                    ps_p[:, :], wk[:, i * 512 + ptt * 128:
                                   i * 512 + ptt * 128 + 128],
                    xc[i][:, :], start=(i == 0), stop=(i == 7)),
                    reads=["wk", f"xc{i}"], writes=["ps_p"])
            p.op("vector", lambda v, ptt=ptt, sl=sl: v.tensor_scalar_add(
                k16[ptt][:, sl], ps_p[:, :], bk[ptt][:, 0:1]),
                reads=["ps_p", f"bk{ptt}"], writes=[f"k16{ptt}"])
        for sub in range(TQ // KT):
            kt = tc * 4 + sub
            for ptt in range(NP):
                p.op("tensor", lambda t, ptt=ptt, kt=kt: t.transpose(
                    ps_t[:, :], k16[ptt][:, kt * KT:(kt + 1) * KT], idb[:, :]),
                    reads=[f"k16{ptt}", "idb"], writes=["ps_t"])
                for hh in range(2):
                    h = ptt * 2 + hh
                    p.op("vector", lambda v, kt=kt, h=h, hh=hh: v.tensor_copy(
                        vt.rearrange("p (k c) -> p k c", c=520)
                          [:, kt, h * 65:h * 65 + 64],
                        ps_t[:, hh * 64:(hh + 1) * 64]),
                        reads=["ps_t"], writes=[f"vt_{kt}_{h}"])

    sc8 = ct.cap(0.125)
    b12 = ct.cap(-12.0)
    for tc in range(NT):
        qsl = slice(tc * TQ, (tc + 1) * TQ)
        for h in range(8):
            ptt, hh = h // 2, h % 2
            for kt in range(NKT):
                ksl = slice(kt * KT, (kt + 1) * KT)
                p.op("tensor", lambda t, ptt=ptt, hh=hh, ksl=ksl, qsl=qsl:
                     t.matmul(ps_s[:, :], k16[ptt][hh * 64:hh * 64 + 64, ksl],
                              k16[ptt][hh * 64:hh * 64 + 64, qsl],
                              start=True, stop=True),
                     reads=[f"k16{ptt}"], writes=["ps_s"])
                p.op("scalar", lambda sc: sc.activation(
                    pexp[:, :], ps_s[:, :], AF.Exp, bias=b12[:, 0:1],
                    scale=sc8[:, 0:1]),
                    reads=["ps_s", "__c"], writes=["pexp"])
                p.op("tensor", lambda t, kt=kt, h=h: t.matmul(
                    ps_o[0:65, :],
                    vt.rearrange("p (k c) -> p k c", c=520)[:, kt,
                                                            h * 65:h * 65 + 65],
                    pexp[:, :], start=(kt == 0), stop=(kt == NKT - 1)),
                    reads=[f"vt_{kt}_{h}", f"vt_ones{h}", "pexp"],
                    writes=["ps_o"])
            p.op("vector", lambda v: v.tensor_copy(dn[:, :], ps_o[64:65, :]),
                 reads=["ps_o"], writes=["dn"])
            p.op("vector", lambda v: v.reciprocal(rdn[:, :], dn[:, :]),
                 reads=["dn"], writes=["rdn"])
            p.op("tensor", lambda t: t.matmul(
                ps_b[:, :], ones1f[0:1, 0:64], rdn[:, :], start=True, stop=True),
                reads=["rdn", "ones1f"], writes=["ps_b"])
            p.op("scalar", lambda sc: sc.copy(dbc[:, :], ps_b[:, :]),
                 reads=["ps_b"], writes=["dbc"])
            p.op("vector", lambda v: v.tensor_mul(
                ao_sb[:, :], ps_o[0:64, :], dbc[:, :]),
                reads=["ps_o", "dbc"], writes=["ao_sb"])
            p.dma("dh", lambda s, h=h, qsl=qsl: s.dma_start(
                out=AO1[h * 64:(h + 1) * 64, qsl], in_=ao_sb[:, :]),
                reads=["ao_sb"], writes=[f"AO_{tc}_{h}"])

    return ct.done()


def a2_inputs(x2, W_attn, b_attn):
    ident = np.eye(128, dtype=np.float32)
    maps = []
    for c in range(NCORES):
        b, hg = c // 2, c % 2
        csl = slice(D + hg * 512, D + (hg + 1) * 512)
        maps.append({
            "xT": np.ascontiguousarray(x2[b].T),
            "wkT": np.ascontiguousarray(W_attn[:, csl]),
            "bk": np.ascontiguousarray(b_attn[csl][:, None]),
            "identI": ident,
        })
    return maps




def make_ln_scratch2(ct, P):
    s = {}
    s["sq"] = ct.sb("lnsq", [128, 512], BF16)
    s["xc"] = ct.sb("lnxc", [128, 512], BF16)
    s["st"] = ct.sb("lnst", [1, 4 * 512], F32)
    s["ones128b"] = ct.sb("ones128b", [128, 1], BF16)
    ct.p.op("vector", lambda v: v.memset(s["ones128b"][:, :], 1.0),
            writes=["ones128b"])
    s["ps1"], s["ps2"], s["psa"], s["psb"] = (
        P[0][0:1, :], P[1][0:1, :], P[2], P[3])
    s["k1"], s["k2"], s["ka"], s["kb"] = "P0", "P1", "P2", "P3"
    return s


def emit_ln2(ct, xf, xkeys, s, W, xb16=None, xb16keys=None):
    p = ct.p
    for c in range(W // 512):
        sl = slice(c * 512, (c + 1) * 512)
        for i in range(8):
            p.op("vector", lambda v, i=i, sl=sl: v.tensor_copy(
                s["xc"][:, :], xf[i][:, sl]),
                reads=[xkeys[i]], writes=["lnxc"])
            p.op("scalar", lambda sc, i=i, sl=sl: sc.activation(
                s["sq"][:, :], xf[i][:, sl], AF.Square),
                reads=[xkeys[i]], writes=["lnsq"])
            p.op("tensor", lambda t, i=i: t.matmul(
                s["ps1"], s["ones128b"][:, :], s["xc"][:, :],
                start=(i == 0), stop=(i == 7)),
                reads=["ones128b", "lnxc"], writes=[s["k1"]])
            p.op("tensor", lambda t, i=i: t.matmul(
                s["ps2"], s["ones128b"][:, :], s["sq"][:, :],
                start=(i == 0), stop=(i == 7)),
                reads=["ones128b", "lnsq"], writes=[s["k2"]])
        st = s["st"]
        p.op("scalar", lambda sc: sc.activation(
            st[:, 0:512], s["ps1"], AF.Copy, scale=1.0 / 1024),
            reads=[s["k1"]], writes=["ln_mean"])
        p.op("scalar", lambda sc: sc.activation(
            st[:, 512:1024], s["ps2"], AF.Copy, scale=1.0 / 1024),
            reads=[s["k2"]], writes=["ln_var"])
        p.op("scalar", lambda sc: sc.activation(
            st[:, 1024:1536], st[:, 0:512], AF.Square),
            reads=["ln_mean"], writes=["ln_m2"])
        p.op("vector", lambda v: v.tensor_sub(
            st[:, 512:1024], st[:, 512:1024], st[:, 1024:1536]),
            reads=["ln_var", "ln_m2"], writes=["ln_var"])
        eps = ct.cap(1e-5)
        p.op("scalar", lambda sc, eps=eps: sc.activation(
            st[:, 1024:1536], st[:, 512:1024], AF.Sqrt, bias=eps[0:1, :]),
            reads=["ln_var", "__c"], writes=["ln_sd"])
        p.op("vector", lambda v: v.reciprocal(
            st[:, 1024:1536], st[:, 1024:1536]),
            reads=["ln_sd"], writes=["ln_rstd"])
        p.op("vector", lambda v: v.tensor_mul(
            st[:, 1536:2048], st[:, 0:512], st[:, 1024:1536]),
            reads=["ln_mean", "ln_rstd"], writes=["ln_mrs"])
        p.op("vector", lambda v: v.tensor_scalar_mul(
            st[:, 1536:2048], st[:, 1536:2048], -1.0),
            reads=["ln_mrs"], writes=["ln_mrs"])
        p.op("tensor", lambda t: t.matmul(
            s["psa"][:, :], ct._ones1f[0:1, :], st[:, 1024:1536],
            start=True, stop=True),
            reads=["ln_rstd", "ones1f"], writes=[s["ka"]])
        p.op("tensor", lambda t: t.matmul(
            s["psb"][:, :], ct._ones1f[0:1, :], st[:, 1536:2048],
            start=True, stop=True),
            reads=["ln_mrs", "ones1f"], writes=[s["kb"]])
        for i in range(8):
            p.op("vector", lambda v, i=i, sl=sl: v.tensor_mul(
                xf[i][:, sl], xf[i][:, sl], s["psa"][:, :]),
                reads=[xkeys[i], s["ka"]], writes=[xkeys[i]])
            p.op("vector", lambda v, i=i, sl=sl: v.tensor_add(
                xf[i][:, sl], xf[i][:, sl], s["psb"][:, :]),
                reads=[xkeys[i], s["kb"]], writes=[xkeys[i]])
            if xb16 is not None:
                p.op("vector", lambda v, i=i, sl=sl: v.tensor_copy(
                    xb16[i][:, sl], xf[i][:, sl]),
                    reads=[xkeys[i]], writes=[xb16keys[i]])


def emit_round(ct, P, x1, ti, rr, dd, src_ap, rs_col, m1bc, out_ap, s_col,
               rkeys, src_keys, out_key, out_op=None):
    p = ct.p
    k = rkeys
    p.op("vector", lambda v: v.scalar_tensor_tensor(
        x1, src_ap, rs_col, m1bc.broadcast_to(list(x1.shape)), OP.mult, OP.add),
        reads=src_keys + ["__c"], writes=[k + "x1"])
    p.op("vector", lambda v: v.tensor_copy(ti, x1),
         reads=[k + "x1"], writes=[k + "ti"])
    p.op("vector", lambda v: v.tensor_copy(rr, ti),
         reads=[k + "ti"], writes=[k + "rr"])
    p.op("vector", lambda v: v.tensor_sub(dd, x1, rr),
         reads=[k + "x1", k + "rr"], writes=[k + "dd"])
    p.op("vector", lambda v: v.scalar_tensor_tensor(
        rr, dd, 0.5, rr, OP.is_ge, OP.add),
        reads=[k + "dd", k + "rr"], writes=[k + "rr"])
    p.op("vector", lambda v: v.scalar_tensor_tensor(
        rr, dd, -0.5, rr, OP.is_ge, OP.add),
        reads=[k + "dd", k + "rr"], writes=[k + "rr"])
    p.op("vector", lambda v: v.tensor_scalar(
        out_ap, rr, s_col, None, OP.mult),
        reads=[k + "rr"], writes=[out_key])



def build_A3(dbg=False):
    ct = Ctx("A3")
    p = ct.p
    xTh = ct.din("xTh", [D, TH])
    AOh = ct.din("AOh", [D, TH])
    WpI = ct.din("Wp", [D, D])
    bpI = ct.din("bp", [128, 8])
    WfTI = ct.din("WfT", [D, 4096])
    bfI = ct.din("bf", [128, 32])
    AfTI = ct.din("AfT", [D, 16])
    BfTI = ct.din("BfT", [16, 4096])
    WpT2I = ct.din("WpT2", [4096, D])
    bp2I = ct.din("bp2", [128, 8])
    ApTI = ct.din("ApT", [4096, 16])
    BpTI = ct.din("BpT", [16, D])
    X4 = ct.dout("X4", [D, TH])
    if dbg:
        X3D = ct.dout("X3D", [D, TH])
    wfq_d = ct.nc.dram_tensor("wfq_d", [D, 4096], BF16).ap()
    wpq_d = ct.nc.dram_tensor("wpq_d", [4096, D], BF16).ap()
    mx_d = ct.nc.dram_tensor("mx_d", [1, 128], F32).ap()

    P = [ct.ps(f"P{i}", [128, 512]) for i in range(8)]
    xf = [ct.sb(f"xf{i}", [128, TH], F32) for i in range(8)]
    xb16 = [ct.sb(f"xb{i}", [128, TH], BF16) for i in range(8)]
    wbig = ct.sb("wbig", [128, 16 * 1024], BF16)
    wsm = ct.sb("wsm", [128, 1024], BF16)
    h16 = ct.sb("h16", [128, 32 * 512], BF16)
    r16 = ct.sb("r16", [16, 512], BF16)
    lor = ct.sb("lor", [16, 4096], BF16)
    lorA = ct.sb("lorA", [128, 8 * 16], BF16)
    lorAp = ct.sb("lorAp", [128, 32 * 16], BF16)
    lorBp = ct.sb("lorBp", [16, 1024], BF16)
    bcol = ct.sb("bcol", [128, 40], F32)
    bcol2 = ct.sb("bcol2", [128, 8], F32)
    qx1 = ct.sb("qx1", [128, 512], F32)
    qti = ct.sb("qti", [128, 512], I32)
    qrr = ct.sb("qrr", [128, 512], F32)
    qdd = ct.sb("qdd", [128, 512], F32)
    qw = ct.sb("qw", [128, 512], F32)
    qcol = ct.sb("qcol", [128, 4], F32)
    mrow = ct.sb("mrow", [1, 128], F32)
    q16o = ct.sb("q16o", [128, 512], BF16)

    ct.begin()
    ones1f = load_ones(ct)
    ct._ones1f = ones1f
    lns = make_ln_scratch2(ct, P)

    for i in range(8):
        p.dma("dg", lambda g, i=i: g.dma_start(
            out=xf[i][:, :], in_=xTh[i * 128:(i + 1) * 128, :]),
            writes=[f"xf{i}"])
    p.dma("dg", lambda g: g.dma_start(out=bcol[:, 0:8], in_=bpI),
          writes=["bcol"])
    p.dma("dg", lambda g: g.dma_start(out=bcol[:, 8:40], in_=bfI),
          writes=["bcol"])
    p.dma("dg", lambda g: g.dma_start(out=bcol2[:, 0:8], in_=bp2I),
          writes=["bcol2"])
    p.dma("dg", lambda g: g.dma_start(
        out=lorA.rearrange("p (i c) -> p i c", c=16),
        in_=AfTI.rearrange("(i p) c -> p i c", p=128)), writes=["lorA"])
    p.dma("dg", lambda g: g.dma_start(out=lor[:, :], in_=BfTI),
          writes=["lor"])
    p.dma("dg", lambda g: g.dma_start(
        out=lorAp.rearrange("p (i c) -> p i c", c=16),
        in_=ApTI.rearrange("(i p) c -> p i c", p=128)), writes=["lorAp"])
    p.dma("dg", lambda g: g.dma_start(out=lorBp[:, :], in_=BpTI),
          writes=["lorBp"])

    for do in range(8):
        p.dma("dg", lambda g, do=do: g.dma_start(
            out=wbig[:, 0:1024].rearrange("p (k c) -> p k c", c=128),
            in_=WpI[:, do * 128:(do + 1) * 128]
                .rearrange("(k p) c -> p k c", p=128)),
            writes=["wbig"])
        for tch in range(TH // 512):
            sl = slice(tch * 512, (tch + 1) * 512)
            for kt in range(8):
                p.dma("dg", lambda g, kt=kt, sl=sl: g.dma_start(
                    out=q16o[:, :], in_=AOh[kt * 128:(kt + 1) * 128, sl]),
                    writes=["q16o"])
                p.op("tensor", lambda t, kt=kt: t.matmul(
                    P[4][:, :], wbig[:, kt * 128:(kt + 1) * 128], q16o[:, :],
                    start=(kt == 0), stop=(kt == 7)),
                    reads=["wbig", "q16o"], writes=["P4"])
            p.op("vector", lambda v, do=do: v.tensor_scalar_add(
                qw[:, :], P[4][:, :], bcol[:, do:do + 1]),
                reads=["P4", "bcol"], writes=["qw_t"])
            p.op("vector", lambda v, do=do, sl=sl: v.tensor_add(
                xf[do][:, sl], xf[do][:, sl], qw[:, :]),
                reads=[f"xf{do}", "qw_t"], writes=[f"xf{do}"])

    xkeys = [f"xf{i}" for i in range(8)]
    xbkeys = [f"xb{i}" for i in range(8)]
    emit_ln2(ct, xf, xkeys, lns, TH, xb16, xbkeys)
    if dbg:
        for i in range(8):
            p.dma("dh", lambda s, i=i: s.dma_start(
                out=X3D[i * 128:(i + 1) * 128, :], in_=xf[i][:, :]),
                reads=[f"xf{i}"], writes=[f"X3D{i}"])

    m1bc = ct.cap(-1.0)
    for wi, (W, Wq, nrow, name) in enumerate(
            ((WfTI, wfq_d, D, "wf"), (WpT2I, wpq_d, 4096, "wp"))):
        nrt = nrow // 128
        ncch = (W.shape[1]) // 512
        for rt in range(nrt):
            for cc in range(ncch):
                p.dma("dg", lambda g, rt=rt, cc=cc, W=W: g.dma_start(
                    out=qw[:, :], in_=W[rt * 128:(rt + 1) * 128,
                                        cc * 512:(cc + 1) * 512]),
                    writes=["qw"])
                p.op("vector", lambda v: v.reduce_max(
                    qcol[:, 1:2], qw[:, :], axis=AX.X,
                    apply_absolute_value=True),
                    reads=["qw"], writes=["qc1"])
                if rt == 0 and cc == 0:
                    p.op("vector", lambda v: v.tensor_copy(
                        qcol[:, 0:1], qcol[:, 1:2]),
                        reads=["qc1"], writes=["qc0"])
                else:
                    p.op("vector", lambda v: v.tensor_max(
                        qcol[:, 0:1], qcol[:, 0:1], qcol[:, 1:2]),
                        reads=["qc0", "qc1"], writes=["qc0"])
        p.dma("dg", lambda g: g.dma_start(
            out=mx_d, in_=qcol[:, 0:1]),
            reads=["qc0"], writes=["mx_d"])
        p.dma("dg", lambda g: g.dma_start(out=mrow[:, :], in_=mx_d),
              reads=["mx_d"], writes=["mrow"])
        p.op("vector", lambda v: v.reduce_max(
            mrow[:, 0:1], mrow[:, :], axis=AX.X),
            reads=["mrow"], writes=["mmax"])
        p.op("vector", lambda v: v.tensor_scalar_mul(
            mrow[:, 1:2], mrow[:, 0:1], 1.0 / 127.0),
            reads=["mmax"], writes=["ms"])
        p.op("vector", lambda v: v.reciprocal(
            mrow[:, 2:3], mrow[:, 1:2]), reads=["ms"], writes=["mrs"])
        p.op("tensor", lambda t: t.matmul(
            P[4][:, 0:2], ones1f[0:1, :], mrow[:, 1:3], start=True, stop=True),
            reads=["ms", "mrs", "ones1f"], writes=["P4"])
        p.op("vector", lambda v: v.tensor_copy(qcol[:, 2:4], P[4][:, 0:2]),
             reads=["P4"], writes=["qsrs"])
        for rt in range(nrt):
            for cc in range(ncch):
                p.dma("dg", lambda g, rt=rt, cc=cc, W=W: g.dma_start(
                    out=qw[:, :], in_=W[rt * 128:(rt + 1) * 128,
                                        cc * 512:(cc + 1) * 512]),
                    writes=["qw"])
                emit_round(ct, P, qx1[:, :], qti[:, :], qrr[:, :], qdd[:, :],
                           qw[:, :], qcol[:, 3:4], m1bc, q16o[:, :],
                           qcol[:, 2:3], "q_", ["qw", "qsrs"], "q16o")
                p.dma("dg", lambda g, rt=rt, cc=cc, Wq=Wq: g.dma_start(
                    out=Wq[rt * 128:(rt + 1) * 128, cc * 512:(cc + 1) * 512],
                    in_=q16o[:, :]),
                    reads=["q16o"], writes=[f"{name}q_d"])

    for tch in range(TH // 512):
        sl = slice(tch * 512, (tch + 1) * 512)
        for i in range(8):
            p.op("tensor", lambda t, i=i, sl=sl: t.matmul(
                P[5][0:16, :], lorA[:, i * 16:(i + 1) * 16], xb16[i][:, sl],
                start=(i == 0), stop=(i == 7)),
                reads=["lorA", f"xb{i}"], writes=["P5"])
        p.op("vector", lambda v: v.tensor_scalar_mul(
            r16[:, :], P[5][0:16, :], 2.0), reads=["P5"], writes=["r16"])
        for half in range(2):
            p.dma("dg", lambda g, half=half: g.dma_start(
                out=wbig.rearrange("p (i c) -> p i c", c=2048)[:, :, :],
                in_=wfq_d[:, half * 2048:(half + 1) * 2048]
                    .rearrange("(i p) c -> p i c", p=128)),
                reads=["wfq_d"], writes=["wbig"])
            for fth in range(16):
                ft = half * 16 + fth
                for i in range(8):
                    p.op("tensor", lambda t, i=i, fth=fth, sl=sl: t.matmul(
                        P[4][:, :], wbig[:, i * 2048 + fth * 128:
                                         i * 2048 + fth * 128 + 128],
                        xb16[i][:, sl], start=(i == 0), stop=False),
                        reads=["wbig", f"xb{i}"], writes=["P4"])
                p.op("tensor", lambda t, ft=ft: t.matmul(
                    P[4][:, :], lor[:, ft * 128:(ft + 1) * 128], r16[:, :],
                    start=False, stop=True),
                    reads=["lor", "r16"], writes=["P4"])
                p.op("scalar", lambda sc, ft=ft: sc.activation(
                    h16[:, ft * 512:(ft + 1) * 512], P[4][:, :], AF.Gelu,
                    bias=bcol[:, 8 + ft:9 + ft]),
                    reads=["P4", "bcol"], writes=["h16"])
        for ft in range(32):
            p.op("tensor", lambda t, ft=ft: t.matmul(
                P[4][0:16, :], lorAp[:, ft * 16:(ft + 1) * 16],
                h16[:, ft * 512:(ft + 1) * 512],
                start=(ft == 0), stop=(ft == 31)),
                reads=["lorAp", "h16"], writes=["P4"])
        p.op("vector", lambda v: v.tensor_scalar_mul(
            r16[:, :], P[4][0:16, :], 2.0), reads=["P4"], writes=["r16"])
        for ft in range(32):
            p.dma("dg", lambda g, ft=ft: g.dma_start(
                out=wsm[:, :], in_=wpq_d[ft * 128:(ft + 1) * 128, :]),
                reads=["wpq_d"], writes=["wsm"])
            for do in range(8):
                p.op("tensor", lambda t, ft=ft, do=do: t.matmul(
                    P[do][:, :], wsm[:, do * 128:(do + 1) * 128],
                    h16[:, ft * 512:(ft + 1) * 512],
                    start=(ft == 0), stop=False),
                    reads=["wsm", "h16"], writes=[f"P{do}"])
        for do in range(8):
            p.op("tensor", lambda t, do=do: t.matmul(
                P[do][:, :], lorBp[:, do * 128:(do + 1) * 128], r16[:, :],
                start=False, stop=True),
                reads=["lorBp", "r16"], writes=[f"P{do}"])
            p.op("vector", lambda v, do=do: v.tensor_scalar_add(
                qw[:, :], P[do][:, :], bcol2[:, do:do + 1]),
                reads=[f"P{do}", "bcol2"], writes=["qw_t"])
            p.op("vector", lambda v, do=do, sl=sl: v.tensor_add(
                xf[do][:, sl], xf[do][:, sl], qw[:, :]),
                reads=[f"xf{do}", "qw_t"], writes=[f"xf{do}"])

    emit_ln2(ct, xf, xkeys, lns, TH)
    for i in range(8):
        p.dma("dh", lambda s, i=i: s.dma_start(
            out=X4[i * 128:(i + 1) * 128, :], in_=xf[i][:, :]),
            reads=[f"xf{i}"], writes=[f"X4_{i}"])

    return ct.done()


def a3_inputs(x2, ao1, I):
    maps = []
    for c in range(NCORES):
        b, j = c // 2, c % 2
        sl = slice(j * TH, (j + 1) * TH)
        maps.append({
            "xTh": np.ascontiguousarray(x2[b].T[:, sl]),
            "AOh": np.ascontiguousarray(ao1[b].T[:, sl]),
            "Wp": I["L1_attn_W_proj"],
            "bp": np.ascontiguousarray(I["L1_attn_b_proj"].reshape(8, 128).T),
            "WfT": np.ascontiguousarray(I["L1_ffn_W_fc"].T),
            "bf": np.ascontiguousarray(I["L1_ffn_b_fc"].reshape(32, 128).T),
            "AfT": np.ascontiguousarray(I["L1_ffn_A_fc"].T),
            "BfT": np.ascontiguousarray(I["L1_ffn_B_fc"].T),
            "WpT2": np.ascontiguousarray(I["L1_ffn_W_proj"].T),
            "bp2": np.ascontiguousarray(I["L1_ffn_b_proj"].reshape(8, 128).T),
            "ApT": np.ascontiguousarray(I["L1_ffn_A_proj"].T),
            "BpT": np.ascontiguousarray(I["L1_ffn_B_proj"].T),
        })
    return maps




def build_A4():
    ct = Ctx("A4")
    p = ct.p
    TE = TH + 3
    xTh3 = ct.din("xTh3", [D, TE])
    ipTI = ct.din("ipT", [D, 4096])
    cwI = ct.din("cw", [128, 64])
    cbI = ct.din("cb", [128, 16])
    xpTI = ct.din("xpT", [2048, 96])
    dpTI = ct.din("dpT", [64, 2048])
    dtbI = ct.din("dtb", [128, 16])
    DEL = ct.dout("DEL", [2048, TH])
    XC = ct.dout("XC", [2048, TH])
    ZZ = ct.dout("ZZ", [2048, TH])
    BC = ct.dout("BC", [32, TH])

    P = [ct.ps(f"P{i}", [128, 512]) for i in range(8)]
    xc = [ct.sb(f"xc{i}", [128, 512], BF16) for i in range(8)]
    ip16 = ct.sb("ip16", [128, 8 * 4096], BF16)
    xmb = ct.sb("xmb", [128, 16 * 515], F32)
    cvt = ct.sb("cvt", [128, 512], F32)
    xcv16 = ct.sb("xcv16", [128, 16 * 512], BF16)
    xp16 = ct.sb("xp16", [128, 16 * 96], BF16)
    dp16 = ct.sb("dp16", [64, 2048], BF16)
    dt16 = ct.sb("dt16", [64, 512], BF16)
    cw = ct.sb("cwsb", [128, 64], F32)
    cb = ct.sb("cbsb", [128, 16], F32)
    dtb = ct.sb("dtbsb", [128, 16], F32)
    ez = ct.sb("ez", [128, 512], F32)
    bco = ct.sb("bco", [32, 512], F32)

    ct.begin()
    ones1f = load_ones(ct)
    ct._ones1f = ones1f

    p.dma("dg", lambda g: g.dma_start(
        out=ip16.rearrange("p (i c) -> p i c", c=4096),
        in_=ipTI.rearrange("(i p) c -> p i c", p=128)), writes=["ip16"])
    p.dma("dg", lambda g: g.dma_start(
        out=xp16.rearrange("p (i c) -> p i c", c=96),
        in_=xpTI.rearrange("(i p) c -> p i c", p=128)), writes=["xp16"])
    p.dma("dg", lambda g: g.dma_start(out=dp16[:, :], in_=dpTI),
          writes=["dp16"])
    p.dma("dg", lambda g: g.dma_start(out=cw[:, :], in_=cwI), writes=["cw"])
    p.dma("dg", lambda g: g.dma_start(out=cb[:, :], in_=cbI), writes=["cb"])
    p.dma("dg", lambda g: g.dma_start(out=dtb[:, :], in_=dtbI), writes=["dtb"])
    for i in range(8):
        p.dma("dg", lambda g, i=i: g.dma_start(
            out=xc[i][:, 0:3], in_=xTh3[i * 128:(i + 1) * 128, 0:3]),
            writes=[f"xc{i}"])
    for ot in range(16):
        for i in range(8):
            p.op("tensor", lambda t, i=i, ot=ot: t.matmul(
                P[0][:, 0:3], ip16[:, i * 4096 + ot * 128:
                                   i * 4096 + ot * 128 + 128],
                xc[i][:, 0:3], start=(i == 0), stop=(i == 7)),
                reads=["ip16", f"xc{i}"], writes=["P0"])
        p.op("vector", lambda v, ot=ot: v.tensor_copy(
            xmb[:, ot * 515:ot * 515 + 3], P[0][:, 0:3]),
            reads=["P0"], writes=[f"xm3_{ot}"])

    NCH = TH // 512
    for ch in range(NCH):
        for i in range(8):
            p.dma("dg", lambda g, i=i, ch=ch: g.dma_start(
                out=xc[i][:, :], in_=xTh3[i * 128:(i + 1) * 128,
                                          3 + ch * 512:3 + (ch + 1) * 512]),
                writes=[f"xc{i}"])
        for half, start in ((0, 0), (1, 2048)):
            for ot in range(16):
                for i in range(8):
                    p.op("tensor", lambda t, i=i, ot=ot, start=start: t.matmul(
                        P[ot % 4][:, :],
                        ip16[:, i * 4096 + start + ot * 128:
                             i * 4096 + start + ot * 128 + 128],
                        xc[i][:, :], start=(i == 0), stop=(i == 7)),
                        reads=["ip16", f"xc{i}"], writes=[f"P{ot % 4}"])
                if half == 0:
                    p.op("vector", lambda v, ot=ot: v.tensor_copy(
                        xmb[:, ot * 515 + 3:ot * 515 + 515], P[ot % 4][:, :]),
                        reads=[f"P{ot % 4}"], writes=[f"xm_{ot}"])
                else:
                    p.op("vector", lambda v, ot=ot: v.tensor_copy(
                        cvt[:, :], P[ot % 4][:, :]),
                        reads=[f"P{ot % 4}"], writes=["cvt"])
                    p.dma("dh", lambda s, ot=ot, ch=ch: s.dma_start(
                        out=ZZ[ot * 128:(ot + 1) * 128,
                               ch * 512:(ch + 1) * 512], in_=cvt[:, :]),
                        reads=["cvt"], writes=[f"ZZo{ot}"])
        for ot in range(16):
            p.op("vector", lambda v, ot=ot: v.tensor_scalar(
                cvt[:, :], xmb[:, ot * 515 + 0:ot * 515 + 512],
                cw[:, ot * 4 + 0:ot * 4 + 1], None, OP.mult),
                reads=[f"xm_{ot}", f"xm3_{ot}", "cw"], writes=["cvt"])
            for w in range(1, 4):
                p.op("vector", lambda v, ot=ot, w=w: v.scalar_tensor_tensor(
                    cvt[:, :], xmb[:, ot * 515 + w:ot * 515 + w + 512],
                    cw[:, ot * 4 + w:ot * 4 + w + 1], cvt[:, :],
                    OP.mult, OP.add),
                    reads=[f"xm_{ot}", f"xm3_{ot}", "cw", "cvt"],
                    writes=["cvt"])
            p.op("scalar", lambda sc, ot=ot: sc.activation(
                cvt[:, :], cvt[:, :], AF.Silu, bias=cb[:, ot:ot + 1]),
                reads=["cvt", "cb"], writes=["cvt"])
            p.op("vector", lambda v, ot=ot: v.tensor_copy(
                xcv16[:, ot * 512:(ot + 1) * 512], cvt[:, :]),
                reads=["cvt"], writes=["xcv16"])
            p.dma("dh", lambda s, ot=ot, ch=ch: s.dma_start(
                out=XC[ot * 128:(ot + 1) * 128, ch * 512:(ch + 1) * 512],
                in_=cvt[:, :]),
                reads=["cvt"], writes=[f"XCo{ot}"])
            p.op("vector", lambda v, ot=ot: v.tensor_copy(
                xmb[:, ot * 515:ot * 515 + 3],
                xmb[:, ot * 515 + 512:ot * 515 + 515]),
                reads=[f"xm_{ot}"], writes=[f"xm3_{ot}"])
        for i in range(16):
            p.op("tensor", lambda t, i=i: t.matmul(
                P[4][0:96, :], xp16[:, i * 96:(i + 1) * 96],
                xcv16[:, i * 512:(i + 1) * 512],
                start=(i == 0), stop=(i == 15)),
                reads=["xp16", "xcv16"], writes=["P4"])
        p.op("vector", lambda v: v.tensor_copy(bco[:, :], P[4][64:96, :]),
             reads=["P4"], writes=["bco"])
        p.dma("dh", lambda s, ch=ch: s.dma_start(
            out=BC[:, ch * 512:(ch + 1) * 512], in_=bco[:, :]),
            reads=["bco"], writes=["BCo"])
        p.op("vector", lambda v: v.tensor_copy(dt16[:, :], P[4][0:64, :]),
             reads=["P4"], writes=["dt16"])
        for ot in range(16):
            p.op("tensor", lambda t, ot=ot: t.matmul(
                P[5][:, :], dp16[:, ot * 128:(ot + 1) * 128], dt16[:, :],
                start=True, stop=True),
                reads=["dp16", "dt16"], writes=["P5"])
            p.op("scalar", lambda sc, ot=ot: sc.activation(
                ez[:, :], P[5][:, :], AF.Exp, bias=dtb[:, ot:ot + 1]),
                reads=["P5", "dtb"], writes=["ez"])
            p.op("scalar", lambda sc: sc.activation(
                ez[:, :], ez[:, :], AF.Ln, bias=1.0),
                reads=["ez"], writes=["ez"])
            p.dma("dh", lambda s, ot=ot, ch=ch: s.dma_start(
                out=DEL[ot * 128:(ot + 1) * 128, ch * 512:(ch + 1) * 512],
                in_=ez[:, :]),
                reads=["ez"], writes=[f"DELo{ot}"])

    return ct.done()


def a4_inputs(x4, I):
    ip = I["L2_attn_in_proj_w"]
    maps = []
    for c in range(NCORES):
        b, j = c // 2, c % 2
        lo = j * TH - 3
        if lo < 0:
            sl = np.concatenate([np.zeros((D, 3), np.float32),
                                 x4[b].T[:, 0:TH]], axis=1)
        else:
            sl = x4[b].T[:, lo:(j + 1) * TH]
        maps.append({
            "xTh3": np.ascontiguousarray(sl),
            "ipT": np.ascontiguousarray(ip.T),
            "cw": np.ascontiguousarray(
                I["L2_attn_conv_w"].reshape(16, 128, 4).transpose(1, 0, 2)
                .reshape(128, 64)),
            "cb": np.ascontiguousarray(
                I["L2_attn_conv_b"].reshape(16, 128).T),
            "xpT": np.ascontiguousarray(I["L2_attn_x_proj_w"].T),
            "dpT": np.ascontiguousarray(I["L2_attn_dt_proj_w"].T),
            "dtb": np.ascontiguousarray(
                I["L2_attn_dt_proj_b"].reshape(16, 128).T),
        })
    return maps




def build_A4b():
    ct = Ctx("A4b")
    p = ct.p
    DELs = ct.din("DELs", [1024, T])
    XCs = ct.din("XCs", [1024, T])
    ZZs = ct.din("ZZs", [1024, T])
    BCf = ct.din("BCf", [32, T])
    ALsI = ct.din("ALs", [128, 8 * 16])
    DsI = ct.din("Ds", [128, 8])
    opTsI = ct.din("opTs", [1024, D])
    wfrI = ct.din("wfr", [D, D])
    wfiI = ct.din("wfi", [D, D])
    tCI = ct.din("tC", [D, 6 * 128])
    OPp = ct.dout("OPp", [D, T])
    PTs = ct.dout("PTs", [D, 128])
    QTs = ct.dout("QTs", [D, 128])

    P = [ct.ps(f"P{i}", [128, 512]) for i in range(8)]
    als = ct.sb("als", [128, 8 * 16], F32)
    nea = ct.sb("nea", [128, 8 * 16], F32)
    ds = ct.sb("dssb", [128, 8], F32)
    op16 = ct.sb("op16", [128, 8 * D], BF16)
    brep = ct.sb("brep", [128, 16 * 512], BF16)
    crep = ct.sb("crep", [128, 16 * 512], BF16)
    dc = ct.sb("dc", [128, 512], F32)
    uc = ct.sb("uc", [128, 512], F32)
    zc = ct.sb("zc", [128, 512], F32)
    wc = ct.sb("wc", [128, 512], F32)
    dac = ct.sb("dac", [128, 512], F32)
    wbc = ct.sb("wbc", [128, 512], F32)
    hc = ct.sb("hc", [128, 512], F32)
    yac = ct.sb("yac", [128, 512], F32)
    tmp = ct.sb("tmpc", [128, 512], F32)
    yg16 = ct.sb("yg16", [128, 512], BF16)
    hst = ct.sb("hst", [128, 8 * 16], F32)
    evb = ct.sb("evb", [128, 512], F32)
    wr16 = ct.sb("wr16", [128, 8 * D], BF16)
    wi16 = ct.sb("wi16", [128, 8 * D], BF16)
    tc16 = ct.sb("tc16", [128, 8 * 768], BF16)

    ct.begin()
    ones1f = load_ones(ct)
    ct._ones1f = ones1f

    p.dma("dg", lambda g: g.dma_start(out=als[:, :], in_=ALsI), writes=["als"])
    p.dma("dg", lambda g: g.dma_start(out=ds[:, :], in_=DsI), writes=["ds"])
    p.dma("dg", lambda g: g.dma_start(
        out=op16.rearrange("p (i c) -> p i c", c=D),
        in_=opTsI.rearrange("(i p) c -> p i c", p=128)), writes=["op16"])
    p.op("scalar", lambda sc: sc.activation(nea[:, :], als[:, :], AF.Exp),
         reads=["als"], writes=["nea"])
    p.op("vector", lambda v: v.tensor_scalar_mul(nea[:, :], nea[:, :], -1.0),
         reads=["nea"], writes=["nea"])
    p.op("vector", lambda v: v.memset(hst[:, :], 0.0), writes=["hst"])

    NCH = T // 512
    for ch in range(NCH):
        chs = slice(ch * 512, (ch + 1) * 512)
        p.dma("dg", lambda g, chs=chs: g.dma_start(
            out=brep.rearrange("p (n t) -> p n t", t=512),
            in_=bcast_dram(BCf[0:16, chs])),
            writes=["brep"])
        p.dma("dg", lambda g, chs=chs: g.dma_start(
            out=crep.rearrange("p (n t) -> p n t", t=512),
            in_=bcast_dram(BCf[16:32, chs])),
            writes=["crep"])
        for dt_ in range(8):
            rs = slice(dt_ * 128, (dt_ + 1) * 128)
            p.dma("dg", lambda g, rs=rs, chs=chs: g.dma_start(
                out=dc[:, :], in_=DELs[rs, chs]), writes=["dc"])
            p.dma("dg", lambda g, rs=rs, chs=chs: g.dma_start(
                out=uc[:, :], in_=XCs[rs, chs]), writes=["uc"])
            p.op("vector", lambda v: v.tensor_mul(wc[:, :], dc[:, :], uc[:, :]),
                 reads=["dc", "uc"], writes=["wc"])
            for n in range(16):
                cix = dt_ * 16 + n
                p.op("scalar", lambda sc, cix=cix: sc.activation(
                    dac[:, :], dc[:, :], AF.Exp, scale=nea[:, cix:cix + 1]),
                    reads=["dc", "nea"], writes=["dac"])
                p.op("vector", lambda v, n=n: v.tensor_mul(
                    wbc[:, :], wc[:, :], brep[:, n * 512:(n + 1) * 512]),
                    reads=["wc", "brep"], writes=["wbc"])
                p.op("vector", lambda v, cix=cix: v.tensor_tensor_scan(
                    hc[:, :], dac[:, :], wbc[:, :], hst[:, cix:cix + 1],
                    OP.mult, OP.add),
                    reads=["dac", "wbc", "hst"], writes=["hc"])
                p.op("vector", lambda v, cix=cix: v.tensor_copy(
                    hst[:, cix:cix + 1], hc[:, 511:512]),
                    reads=["hc"], writes=["hst"])
                if n == 0:
                    p.op("vector", lambda v, n=n: v.tensor_mul(
                        yac[:, :], hc[:, :], crep[:, n * 512:(n + 1) * 512]),
                        reads=["hc", "crep"], writes=["yac"])
                else:
                    p.op("vector", lambda v, n=n: v.tensor_mul(
                        tmp[:, :], hc[:, :], crep[:, n * 512:(n + 1) * 512]),
                        reads=["hc", "crep"], writes=["tmpc"])
                    p.op("vector", lambda v: v.tensor_add(
                        yac[:, :], yac[:, :], tmp[:, :]),
                        reads=["yac", "tmpc"], writes=["yac"])
            p.op("vector", lambda v, dt_=dt_: v.scalar_tensor_tensor(
                yac[:, :], uc[:, :], ds[:, dt_:dt_ + 1], yac[:, :],
                OP.mult, OP.add),
                reads=["uc", "ds", "yac"], writes=["yac"])
            p.dma("dg", lambda g, rs=rs, chs=chs: g.dma_start(
                out=zc[:, :], in_=ZZs[rs, chs]), writes=["zc"])
            p.op("scalar", lambda sc: sc.activation(zc[:, :], zc[:, :], AF.Silu),
                 reads=["zc"], writes=["zc"])
            p.op("vector", lambda v: v.tensor_mul(yg16[:, :], yac[:, :], zc[:, :]),
                 reads=["yac", "zc"], writes=["yg16"])
            for do in range(8):
                p.op("tensor", lambda t, dt_=dt_, do=do: t.matmul(
                    P[do][:, :], op16[:, dt_ * D + do * 128:
                                      dt_ * D + do * 128 + 128],
                    yg16[:, :], start=(dt_ == 0), stop=(dt_ == 7)),
                    reads=["op16", "yg16"], writes=[f"P{do}"])
        for do in range(8):
            p.op("vector", lambda v, do=do: v.tensor_copy(
                evb[:, :], P[do][:, :]), reads=[f"P{do}"], writes=["evb"])
            p.dma("dh", lambda s, do=do, chs=chs: s.dma_start(
                out=OPp[do * 128:(do + 1) * 128, chs], in_=evb[:, :]),
                reads=["evb"], writes=[f"OPo{do}"])

    p.dma("dg", lambda g: g.dma_start(
        out=wr16.rearrange("p (i c) -> p i c", c=D),
        in_=wfrI.rearrange("(i p) c -> p i c", p=128)), writes=["wr16"])
    p.dma("dg", lambda g: g.dma_start(
        out=wi16.rearrange("p (i c) -> p i c", c=D),
        in_=wfiI.rearrange("(i p) c -> p i c", p=128)), writes=["wi16"])
    p.dma("dg", lambda g: g.dma_start(
        out=tc16.rearrange("p (i c) -> p i c", c=768),
        in_=tCI.rearrange("(i p) c -> p i c", p=128)), writes=["tc16"])
    for out_d, terms in ((PTs, (("wr16", wr16, 0), ("wr16", wr16, 1),
                                ("wi16", wi16, 2), ("wi16", wi16, 3))),
                         (QTs, (("wi16", wi16, 0), ("wi16", wi16, 1),
                                ("wr16", wr16, 4), ("wr16", wr16, 5)))):
        for ot in range(8):
            nterms = len(terms) * 8
            k = 0
            for wkey, wsb, tb in terms:
                for i in range(8):
                    p.op("tensor",
                         lambda t, wsb=wsb, i=i, ot=ot, tb=tb, k=k, n=nterms:
                         t.matmul(P[0][:, 0:128],
                                  wsb[:, i * D + ot * 128:i * D + ot * 128 + 128],
                                  tc16[:, i * 768 + tb * 128:
                                       i * 768 + tb * 128 + 128],
                                  start=(k == 0), stop=(k == n - 1)),
                         reads=[wkey, "tc16"], writes=["P0"])
                    k += 1
            p.op("vector", lambda v: v.tensor_copy(evb[:, 0:128], P[0][:, 0:128]),
                 reads=["P0"], writes=["evb"])
            p.dma("dh", lambda s, ot=ot, out_d=out_d: s.dma_start(
                out=out_d[ot * 128:(ot + 1) * 128, :], in_=evb[:, 0:128]),
                reads=["evb"], writes=["PTo"])

    return ct.done()


def fft_tables():
    jj = np.arange(D, dtype=np.float64)
    ang = 2.0 * np.pi * np.outer(jj, jj) / D
    prod = (np.outer(jj, jj) % D)
    ang = 2.0 * np.pi * prod / D
    C = np.cos(ang)
    S = np.sin(ang)
    Ch = C.astype(np.float32)
    Cl = (C - Ch).astype(np.float32)
    Sh = S.astype(np.float32)
    Sl = (S - Sh).astype(np.float32)
    return Ch, Cl, Sh, Sl


def a4b_inputs(delF, xcF, zzF, bcF, I):
    Ch, Cl, Sh, Sl = fft_tables()
    AL = I["L2_attn_A_log"]
    Dd = I["L2_attn_D"]
    opT = np.ascontiguousarray(I["L2_attn_out_proj_w"].T)
    maps = []
    for c in range(NCORES):
        b, dh = c // 2, c % 2
        rs = slice(dh * 1024, (dh + 1) * 1024)
        cs = slice(c * 128, (c + 1) * 128)
        tC = np.concatenate([Ch[:, cs], Cl[:, cs], Sh[:, cs], Sl[:, cs],
                             -Sh[:, cs], -Sl[:, cs]], 1)
        maps.append({
            "DELs": np.ascontiguousarray(delF[b, rs]),
            "XCs": np.ascontiguousarray(xcF[b, rs]),
            "ZZs": np.ascontiguousarray(zzF[b, rs]),
            "BCf": np.ascontiguousarray(bcF[b]),
            "ALs": np.ascontiguousarray(
                AL[rs].reshape(8, 128, 16).transpose(1, 0, 2).reshape(128, 128)),
            "Ds": np.ascontiguousarray(Dd[rs].reshape(8, 128).T),
            "opTs": np.ascontiguousarray(opT[rs]),
            "wfr": I["L2_ffn_w_freq_real"],
            "wfi": I["L2_ffn_w_freq_imag"],
            "tC": np.ascontiguousarray(tC),
        })
    return maps




def build_A5():
    ct = Ctx("A5")
    p = ct.p
    xTh = ct.din("xTh", [D, TH])
    OPa = ct.din("OPa", [D, TH])
    OPb = ct.din("OPb", [D, TH])
    PTI = ct.din("PT", [D, D])
    QTI = ct.din("QT", [D, D])
    t2CI = ct.din("t2C", [D, 4 * D])
    fbI = ct.din("fb", [128, 8])
    WdI = ct.din("Wd", [D, D])
    WbcI = ct.din("Wbc", [D, 32])
    X6 = ct.dout("X6", [D, TH])
    DEL3 = ct.dout("DEL3", [D, TH])
    BC3 = ct.dout("BC3", [32, TH])
    MD = ct.dout("MD", [128, 8 * D], BF16)

    P = [ct.ps(f"P{i}", [128, 512]) for i in range(8)]
    xf = [ct.sb(f"xf{i}", [128, TH], F32) for i in range(8)]
    xb16 = [ct.sb(f"xb{i}", [128, TH], BF16) for i in range(8)]
    pt16 = ct.sb("pt16", [128, 8 * D], BF16)
    qt16 = ct.sb("qt16", [128, 8 * D], BF16)
    m16 = ct.sb("m16", [128, 8 * D], BF16)
    tt16 = ct.sb("tt16", [128, 4 * 512], BF16)
    wd16 = ct.sb("wd16", [128, 8 * D], BF16)
    wbc16 = ct.sb("wbc16", [128, 8 * 32], BF16)
    fb = ct.sb("fbsb", [128, 8], F32)
    ev = ct.sb("ev", [128, 512], F32)
    ez = ct.sb("ez", [128, 512], F32)
    bco = ct.sb("bco", [32, 512], F32)

    ct.begin()
    ones1f = load_ones(ct)
    ct._ones1f = ones1f
    lns = make_ln_scratch2(ct, P)

    p.dma("dg", lambda g: g.dma_start(out=fb[:, :], in_=fbI), writes=["fbsb"])
    p.dma("dg", lambda g: g.dma_start(
        out=pt16.rearrange("p (i c) -> p i c", c=D),
        in_=PTI.rearrange("(i p) c -> p i c", p=128)), writes=["pt16"])
    p.dma("dg", lambda g: g.dma_start(
        out=qt16.rearrange("p (i c) -> p i c", c=D),
        in_=QTI.rearrange("(i p) c -> p i c", p=128)), writes=["qt16"])
    p.dma("dg", lambda g: g.dma_start(
        out=wd16.rearrange("p (i c) -> p i c", c=D),
        in_=WdI.rearrange("(i p) c -> p i c", p=128)), writes=["wd16"])
    p.dma("dg", lambda g: g.dma_start(
        out=wbc16.rearrange("p (i c) -> p i c", c=32),
        in_=WbcI.rearrange("(i p) c -> p i c", p=128)), writes=["wbc16"])
    for i in range(8):
        p.dma("dg", lambda g, i=i: g.dma_start(
            out=xf[i][:, :], in_=xTh[i * 128:(i + 1) * 128, :]),
            writes=[f"xf{i}"])

    for i in range(8):
        for src_, key in ((OPa, "opa"), (OPb, "opb")):
            for tch in range(TH // 512):
                sl = slice(tch * 512, (tch + 1) * 512)
                p.dma("dg", lambda g, i=i, sl=sl, src_=src_: g.dma_start(
                    out=ev[:, :], in_=src_[i * 128:(i + 1) * 128, sl]),
                    writes=["ev"])
                p.op("vector", lambda v, i=i, sl=sl: v.tensor_add(
                    xf[i][:, sl], xf[i][:, sl], ev[:, :]),
                    reads=[f"xf{i}", "ev"], writes=[f"xf{i}"])

    xkeys = [f"xf{i}" for i in range(8)]
    xbkeys = [f"xb{i}" for i in range(8)]
    emit_ln2(ct, xf, xkeys, lns, TH, xb16, xbkeys)

    for mch in range(2):
        for ot in range(8):
            p.dma("dg", lambda g, ot=ot, mch=mch: g.dma_start(
                out=tt16.rearrange("p (i c) -> p i c", c=512),
                in_=t2CI[ot * 128:(ot + 1) * 128, :]
                    .rearrange("p (i m) -> p i m", i=4)[:, :, mch * 512:(mch + 1) * 512]),
                writes=["tt16"])
            for jt in range(8):
                for tb, wsb, wkey in ((0, pt16, "pt16"), (1, pt16, "pt16"),
                                      (2, qt16, "qt16"), (3, qt16, "qt16")):
                    k = ot * 4 + tb
                    p.op("tensor",
                         lambda t, wsb=wsb, jt=jt, ot=ot, tb=tb, k=k: t.matmul(
                             P[jt][:, :],
                             wsb[:, ot * D + jt * 128:ot * D + jt * 128 + 128],
                             tt16[:, tb * 512:(tb + 1) * 512],
                             start=(k == 0), stop=(k == 31)),
                         reads=[wkey, "tt16"], writes=[f"P{jt}"])
        for jt in range(8):
            p.op("vector", lambda v, jt=jt, mch=mch: v.tensor_copy(
                m16[:, jt * D + mch * 512:jt * D + (mch + 1) * 512],
                P[jt][:, :]), reads=[f"P{jt}"], writes=["m16"])

    for mo in range(8):
        for tch in range(TH // 512):
            sl = slice(tch * 512, (tch + 1) * 512)
            for jt in range(8):
                p.op("tensor", lambda t, jt=jt, mo=mo, sl=sl: t.matmul(
                    P[4][:, :], m16[:, jt * D + mo * 128:jt * D + mo * 128 + 128],
                    xb16[jt][:, sl], start=(jt == 0), stop=(jt == 7)),
                    reads=["m16", f"xb{jt}"], writes=["P4"])
            p.op("vector", lambda v, mo=mo: v.tensor_scalar_add(
                ev[:, :], P[4][:, :], fb[:, mo:mo + 1]),
                reads=["P4", "fbsb"], writes=["ev"])
            p.op("vector", lambda v, mo=mo, sl=sl: v.tensor_add(
                xf[mo][:, sl], xf[mo][:, sl], ev[:, :]),
                reads=[f"xf{mo}", "ev"], writes=[f"xf{mo}"])

    p.dma("dh", lambda s: s.dma_start(out=MD, in_=m16[:, :]),
          reads=["m16"], writes=["MDo"])
    emit_ln2(ct, xf, xkeys, lns, TH, xb16, xbkeys)
    for i in range(8):
        p.dma("dh", lambda s, i=i: s.dma_start(
            out=X6[i * 128:(i + 1) * 128, :], in_=xf[i][:, :]),
            reads=[f"xf{i}"], writes=[f"X6_{i}"])

    for tch in range(TH // 512):
        sl = slice(tch * 512, (tch + 1) * 512)
        for do in range(8):
            for i in range(8):
                p.op("tensor", lambda t, i=i, do=do, sl=sl: t.matmul(
                    P[4][:, :], wd16[:, i * D + do * 128:i * D + do * 128 + 128],
                    xb16[i][:, sl], start=(i == 0), stop=(i == 7)),
                    reads=["wd16", f"xb{i}"], writes=["P4"])
            p.op("scalar", lambda sc: sc.activation(
                ez[:, :], P[4][:, :], AF.Exp),
                reads=["P4"], writes=["ez"])
            p.op("scalar", lambda sc: sc.activation(
                ez[:, :], ez[:, :], AF.Ln, bias=1.0),
                reads=["ez"], writes=["ez"])
            p.dma("dh", lambda s, do=do, sl=sl: s.dma_start(
                out=DEL3[do * 128:(do + 1) * 128, sl], in_=ez[:, :]),
                reads=["ez"], writes=["DEL3o"])
        for i in range(8):
            p.op("tensor", lambda t, i=i, sl=sl: t.matmul(
                P[5][0:32, :], wbc16[:, i * 32:(i + 1) * 32],
                xb16[i][:, sl], start=(i == 0), stop=(i == 7)),
                reads=["wbc16", f"xb{i}"], writes=["P5"])
        p.op("vector", lambda v: v.tensor_copy(bco[:, :], P[5][0:32, :]),
             reads=["P5"], writes=["bco"])
        p.dma("dh", lambda s, sl=sl: s.dma_start(
            out=BC3[:, sl], in_=bco[:, :]),
            reads=["bco"], writes=["BC3o"])

    return ct.done()


def a5_inputs(x4, opF, PT, QT, I):
    Ch, Cl, Sh, Sl = fft_tables()
    sc = 1.0 / D
    t2 = np.concatenate([Ch * sc, Cl * sc, -Sh * sc, -Sl * sc], 1)
    t2C = np.ascontiguousarray(
        np.stack([Ch * sc, Cl * sc, -Sh * sc, -Sl * sc], 1).reshape(D, 4 * D))
    Wbc = np.concatenate([I["L3_attn_W_B"], I["L3_attn_W_C"]], 1)
    maps = []
    for c in range(NCORES):
        b, j = c // 2, c % 2
        sl = slice(j * TH, (j + 1) * TH)
        maps.append({
            "xTh": np.ascontiguousarray(x4[b].T[:, sl]),
            "OPa": np.ascontiguousarray(opF[b, 0][:, sl]),
            "OPb": np.ascontiguousarray(opF[b, 1][:, sl]),
            "PT": PT, "QT": QT, "t2C": t2C,
            "fb": np.ascontiguousarray(I["L2_ffn_bias"].reshape(8, 128).T),
            "Wd": I["L3_attn_W_delta"],
            "Wbc": np.ascontiguousarray(Wbc),
        })
    return maps




def build_A6():
    ct = Ctx("A6")
    p = ct.p
    DELs = ct.din("DEL3s", [512, T])
    X6s = ct.din("X6s", [512, T])
    BCf = ct.din("BC3f", [32, T])
    ALsI = ct.din("ALs", [128, 4 * 16])
    SSM = ct.dout("SSM", [512, T])

    als = ct.sb("als", [128, 4 * 16], F32)
    nea = ct.sb("nea", [128, 4 * 16], F32)
    brep = ct.sb("brep", [128, 16 * 512], BF16)
    crep = ct.sb("crep", [128, 16 * 512], BF16)
    dc = ct.sb("dc", [128, 512], F32)
    uc = ct.sb("uc", [128, 512], F32)
    wc = ct.sb("wc", [128, 512], F32)
    dac = ct.sb("dac", [128, 512], F32)
    wbc = ct.sb("wbc", [128, 512], F32)
    hc = ct.sb("hc", [128, 512], F32)
    yac = ct.sb("yac", [128, 512], F32)
    tmp = ct.sb("tmpc", [128, 512], F32)
    hst = ct.sb("hst", [128, 4 * 16], F32)

    ct.begin()
    p.dma("dg", lambda g: g.dma_start(out=als[:, :], in_=ALsI), writes=["als"])
    p.op("scalar", lambda sc: sc.activation(nea[:, :], als[:, :], AF.Exp),
         reads=["als"], writes=["nea"])
    p.op("vector", lambda v: v.tensor_scalar_mul(nea[:, :], nea[:, :], -1.0),
         reads=["nea"], writes=["nea"])
    p.op("vector", lambda v: v.memset(hst[:, :], 0.0), writes=["hst"])

    for ch in range(T // 512):
        chs = slice(ch * 512, (ch + 1) * 512)
        p.dma("dg", lambda g, chs=chs: g.dma_start(
            out=brep.rearrange("p (n t) -> p n t", t=512),
            in_=bcast_dram(BCf[0:16, chs])), writes=["brep"])
        p.dma("dg", lambda g, chs=chs: g.dma_start(
            out=crep.rearrange("p (n t) -> p n t", t=512),
            in_=bcast_dram(BCf[16:32, chs])), writes=["crep"])
        for dt_ in range(4):
            rs = slice(dt_ * 128, (dt_ + 1) * 128)
            p.dma("dg", lambda g, rs=rs, chs=chs: g.dma_start(
                out=dc[:, :], in_=DELs[rs, chs]), writes=["dc"])
            p.dma("dg", lambda g, rs=rs, chs=chs: g.dma_start(
                out=uc[:, :], in_=X6s[rs, chs]), writes=["uc"])
            p.op("vector", lambda v: v.tensor_mul(wc[:, :], dc[:, :], uc[:, :]),
                 reads=["dc", "uc"], writes=["wc"])
            for n in range(16):
                cix = dt_ * 16 + n
                p.op("scalar", lambda sc, cix=cix: sc.activation(
                    dac[:, :], dc[:, :], AF.Exp, scale=nea[:, cix:cix + 1]),
                    reads=["dc", "nea"], writes=["dac"])
                p.op("vector", lambda v, n=n: v.tensor_mul(
                    wbc[:, :], wc[:, :], brep[:, n * 512:(n + 1) * 512]),
                    reads=["wc", "brep"], writes=["wbc"])
                p.op("vector", lambda v, cix=cix: v.tensor_tensor_scan(
                    hc[:, :], dac[:, :], wbc[:, :], hst[:, cix:cix + 1],
                    OP.mult, OP.add),
                    reads=["dac", "wbc", "hst"], writes=["hc"])
                p.op("vector", lambda v, cix=cix: v.tensor_copy(
                    hst[:, cix:cix + 1], hc[:, 511:512]),
                    reads=["hc"], writes=["hst"])
                if n == 0:
                    p.op("vector", lambda v, n=n: v.tensor_mul(
                        yac[:, :], hc[:, :], crep[:, n * 512:(n + 1) * 512]),
                        reads=["hc", "crep"], writes=["yac"])
                else:
                    p.op("vector", lambda v, n=n: v.tensor_mul(
                        tmp[:, :], hc[:, :], crep[:, n * 512:(n + 1) * 512]),
                        reads=["hc", "crep"], writes=["tmpc"])
                    p.op("vector", lambda v: v.tensor_add(
                        yac[:, :], yac[:, :], tmp[:, :]),
                        reads=["yac", "tmpc"], writes=["yac"])
            p.dma("dh", lambda s, rs=rs, chs=chs: s.dma_start(
                out=SSM[rs, chs], in_=yac[:, :]),
                reads=["yac"], writes=["SSMo"])

    return ct.done()


def a6_inputs(del3F, x6, bc3F, I):
    AL = I["L3_attn_A_log"]
    maps = []
    for c in range(NCORES):
        b, dh = c // 2, c % 2
        rs = slice(dh * 512, (dh + 1) * 512)
        maps.append({
            "DEL3s": np.ascontiguousarray(del3F[b, rs]),
            "X6s": np.ascontiguousarray(x6[b].T[rs]),
            "BC3f": np.ascontiguousarray(bc3F[b]),
            "ALs": np.ascontiguousarray(
                AL[rs].reshape(4, 128, 16).transpose(1, 0, 2).reshape(128, 64)),
        })
    return maps




def build_A7(dbg=False):
    ct = Ctx("A7")
    p = ct.p
    xTh = ct.din("xTh", [D, TH])
    SSMh = ct.din("SSMh", [D, TH])
    gamI = ct.din("gam", [128, 8])
    betI = ct.din("bet", [128, 8])
    WcI = ct.din("Wc", [D, D])
    csI = ct.din("cs", [1, 1])
    X7 = ct.dout("X7", [D, TH])
    MX = ct.dout("MX", [1, 1])

    P = [ct.ps(f"P{i}", [128, 512]) for i in range(8)]
    xf = [ct.sb(f"xf{i}", [128, TH], F32) for i in range(8)]
    sf = [ct.sb(f"sf{i}", [128, TH], F32) for i in range(8)]
    sb16 = [ct.sb(f"sb{i}", [128, TH], BF16) for i in range(8)]
    wc16 = ct.sb("wc16", [128, 8 * D], BF16)
    gam = ct.sb("gamsb", [128, 8], F32)
    bet = ct.sb("betsb", [128, 8], F32)
    csc = ct.sb("csc", [128, 1], F32)
    cs1 = ct.sb("cs1", [1, 1], F32)
    ev = ct.sb("ev", [128, 512], F32)
    mcol = ct.sb("mcol", [128, 2], F32)
    mrow = ct.sb("mrow", [1, 128], F32)
    mx_d = ct.nc.dram_tensor("mx_d7", [1, 128], F32).ap()

    ct.begin()
    ones1f = load_ones(ct)
    ct._ones1f = ones1f
    lns = make_ln_scratch2(ct, P)

    p.dma("dg", lambda g: g.dma_start(out=gam[:, :], in_=gamI), writes=["gam"])
    p.dma("dg", lambda g: g.dma_start(out=bet[:, :], in_=betI), writes=["bet"])
    p.dma("dg", lambda g: g.dma_start(out=cs1[:, :], in_=csI), writes=["cs1"])
    p.dma("dg", lambda g: g.dma_start(
        out=wc16.rearrange("p (i c) -> p i c", c=D),
        in_=WcI.rearrange("(i p) c -> p i c", p=128)), writes=["wc16"])
    for i in range(8):
        p.dma("dg", lambda g, i=i: g.dma_start(
            out=xf[i][:, :], in_=xTh[i * 128:(i + 1) * 128, :]),
            writes=[f"xf{i}"])
        p.dma("dg", lambda g, i=i: g.dma_start(
            out=sf[i][:, :], in_=SSMh[i * 128:(i + 1) * 128, :]),
            writes=[f"sf{i}"])
    p.op("tensor", lambda t: t.matmul(
        P[4][:, 0:1], ones1f[0:1, :], cs1[:, :], start=True, stop=True),
        reads=["cs1", "ones1f"], writes=["P4"])
    p.op("vector", lambda v: v.tensor_copy(csc[:, :], P[4][:, 0:1]),
         reads=["P4"], writes=["csc"])

    skeys = [f"sf{i}" for i in range(8)]
    sbkeys = [f"sb{i}" for i in range(8)]
    emit_ln2(ct, sf, skeys, lns, TH)
    for i in range(8):
        p.op("vector", lambda v, i=i: v.scalar_tensor_tensor(
            sf[i][:, :], sf[i][:, :], gam[:, i:i + 1],
            bet[:, i:i + 1].broadcast_to([128, TH]), OP.mult, OP.add),
            reads=[f"sf{i}", "gam", "bet"], writes=[f"sf{i}"])
        p.op("vector", lambda v, i=i: v.tensor_copy(sb16[i][:, :], sf[i][:, :]),
             reads=[f"sf{i}"], writes=[f"sb{i}"])

    for do in range(8):
        for tch in range(TH // 512):
            sl = slice(tch * 512, (tch + 1) * 512)
            for i in range(8):
                p.op("tensor", lambda t, i=i, do=do, sl=sl: t.matmul(
                    P[4][:, :], wc16[:, i * D + do * 128:i * D + do * 128 + 128],
                    sb16[i][:, sl], start=(i == 0), stop=(i == 7)),
                    reads=["wc16", f"sb{i}"], writes=["P4"])
            p.op("vector", lambda v: v.tensor_scalar(
                ev[:, :], P[4][:, :], csc[:, 0:1], None, OP.mult),
                reads=["P4", "csc"], writes=["ev"])
            p.op("vector", lambda v, do=do, sl=sl: v.tensor_add(
                xf[do][:, sl], xf[do][:, sl], ev[:, :]),
                reads=[f"xf{do}", "ev"], writes=[f"xf{do}"])

    xkeys = [f"xf{i}" for i in range(8)]
    emit_ln2(ct, xf, xkeys, lns, TH)
    for i in range(8):
        p.dma("dh", lambda s, i=i: s.dma_start(
            out=X7[i * 128:(i + 1) * 128, :], in_=xf[i][:, :]),
            reads=[f"xf{i}"], writes=[f"X7_{i}"])
        p.op("vector", lambda v, i=i: v.reduce_max(
            mcol[:, 1:2], xf[i][:, :], axis=AX.X, apply_absolute_value=True),
            reads=[f"xf{i}"], writes=["mc1"])
        if i == 0:
            p.op("vector", lambda v: v.tensor_copy(mcol[:, 0:1], mcol[:, 1:2]),
                 reads=["mc1"], writes=["mc0"])
        else:
            p.op("vector", lambda v: v.tensor_max(
                mcol[:, 0:1], mcol[:, 0:1], mcol[:, 1:2]),
                reads=["mc0", "mc1"], writes=["mc0"])
    p.dma("dg", lambda g: g.dma_start(out=mx_d, in_=mcol[:, 0:1]),
          reads=["mc0"], writes=["mx_d"])
    p.dma("dg", lambda g: g.dma_start(out=mrow[:, :], in_=mx_d),
          reads=["mx_d"], writes=["mrow"])
    p.op("vector", lambda v: v.reduce_max(mrow[:, 0:1], mrow[:, :], axis=AX.X),
         reads=["mrow"], writes=["mrow"])
    p.dma("dh", lambda s: s.dma_start(out=MX, in_=mrow[:, 0:1]),
          reads=["mrow"], writes=["MXo"])

    return ct.done()


def a7_inputs(x6, ssmF, I):
    maps = []
    for c in range(NCORES):
        b, j = c // 2, c % 2
        sl = slice(j * TH, (j + 1) * TH)
        maps.append({
            "xTh": np.ascontiguousarray(x6[b].T[:, sl]),
            "SSMh": np.ascontiguousarray(ssmF[b][:, sl]),
            "gam": np.ascontiguousarray(I["L3_attn_ln_gamma"].reshape(8, 128).T),
            "bet": np.ascontiguousarray(I["L3_attn_ln_beta"].reshape(8, 128).T),
            "Wc": I["L3_attn_W_ctx"],
            "cs": np.ascontiguousarray(
                np.asarray(I["L3_attn_context_scale"]).reshape(1, 1)),
        })
    return maps




def emit_round_chain(ct, p, src_ap, src_keys, out_ap, out_key, rs_scalar,
                     qx1, qti, qrr, qdd, pref, post_scale=None,
                     rs_is_bcast3=False, sb_ap=None, m1bc=None):
    k = pref
    if rs_is_bcast3:
        p.op("vector", lambda v: v.tensor_mul(qx1, src_ap, rs_scalar),
             reads=src_keys, writes=[k + "x1"])
        p.op("vector", lambda v: v.tensor_scalar_add(qx1, qx1, -1.0),
             reads=[k + "x1"], writes=[k + "x1"])
    else:
        p.op("vector", lambda v: v.scalar_tensor_tensor(
            qx1, src_ap, rs_scalar, m1bc.broadcast_to(list(qx1.shape)),
            OP.mult, OP.add),
            reads=src_keys + ["__c"], writes=[k + "x1"])
    p.op("vector", lambda v: v.tensor_copy(qti, qx1),
         reads=[k + "x1"], writes=[k + "ti"])
    p.op("vector", lambda v: v.tensor_copy(qrr, qti),
         reads=[k + "ti"], writes=[k + "rr"])
    p.op("vector", lambda v: v.tensor_sub(qdd, qx1, qrr),
         reads=[k + "x1", k + "rr"], writes=[k + "dd"])
    p.op("vector", lambda v: v.scalar_tensor_tensor(
        qrr, qdd, 0.5, qrr, OP.is_ge, OP.add),
        reads=[k + "dd", k + "rr"], writes=[k + "rr"])
    p.op("vector", lambda v: v.scalar_tensor_tensor(
        qrr, qdd, -0.5, qrr, OP.is_ge, OP.add),
        reads=[k + "dd", k + "rr"], writes=[k + "rr"])
    if sb_ap is not None:
        p.op("vector", lambda v: v.tensor_mul(out_ap, qrr, sb_ap),
             reads=[k + "rr"] + src_keys, writes=[out_key])
    elif post_scale is not None:
        p.op("vector", lambda v: v.tensor_scalar(
            out_ap, qrr, post_scale, None, OP.mult),
            reads=[k + "rr"] + src_keys, writes=[out_key])
    else:
        p.op("vector", lambda v: v.tensor_copy(out_ap, qrr),
             reads=[k + "rr"], writes=[out_key])


def emit_global_scale(ct, p, mrow, mxI, qmax_inv, sxc, rsxc, P4, mkey):
    p.dma("dg", lambda g: g.dma_start(out=mrow[:, 0:8], in_=mxI),
          writes=[mkey])
    p.op("vector", lambda v: v.reduce_max(mrow[:, 8:9], mrow[:, 0:8], axis=AX.X),
         reads=[mkey], writes=[mkey + "m"])
    p.op("vector", lambda v: v.tensor_scalar_mul(
        mrow[:, 8:9], mrow[:, 8:9], qmax_inv),
        reads=[mkey + "m"], writes=[mkey + "m"])
    p.op("vector", lambda v: v.tensor_scalar_max(
        mrow[:, 8:9], mrow[:, 8:9], 1e-12),
        reads=[mkey + "m"], writes=[mkey + "m"])
    p.op("vector", lambda v: v.reciprocal(mrow[:, 9:10], mrow[:, 8:9]),
         reads=[mkey + "m"], writes=[mkey + "r"])
    p.op("tensor", lambda t: t.matmul(
        P4[:, 0:2], ct._ones1f[0:1, :], mrow[:, 8:10], start=True, stop=True),
        reads=[mkey + "m", mkey + "r", "ones1f"], writes=["P4"])
    p.op("vector", lambda v: v.tensor_copy(sxc, P4[:, 0:1]),
         reads=["P4"], writes=[mkey + "sc"])
    p.op("vector", lambda v: v.tensor_copy(rsxc, P4[:, 1:2]),
         reads=["P4"], writes=[mkey + "rc"])


def build_A8():
    ct = Ctx("A8")
    p = ct.p
    xTh = ct.din("xTh", [D, TH])
    MXI = ct.din("MXall", [1, 8])
    wupI = ct.din("wup", [3072, 1024])
    wgtI = ct.din("wgt", [3072, 1024])
    wdnI = ct.din("wdn", [1024, 3072])
    identI = ct.din("identI", [128, 128])
    UP = ct.dout("UP", [3072, TH], BF16)
    GT = ct.dout("GT", [3072, TH], BF16)
    MU = ct.dout("MU", [1, 1])
    MG = ct.dout("MG", [1, 1])
    WDQT = ct.dout("WDQT", [3072, D], BF16)
    wupqT_d = ct.nc.dram_tensor("wupqT_d", [D, 3072], BF16).ap()
    wgqT_d = ct.nc.dram_tensor("wgqT_d", [D, 3072], BF16).ap()
    mx_d = ct.nc.dram_tensor("mx_d8", [2, 128], F32).ap()

    P = [ct.ps(f"P{i}", [128, 512]) for i in range(4)]
    psT = ct.ps("psT", [128, 128], BF16)
    q716 = [ct.sb(f"q7_{i}", [128, TH], BF16) for i in range(8)]
    qw = ct.sb("qw", [128, 1024], F32)
    qx1 = ct.sb("qx1", [128, 1024], F32)
    qti = ct.sb("qti", [128, 1024], I32)
    qrr = ct.sb("qrr", [128, 1024], F32)
    qdd = ct.sb("qdd", [128, 1024], F32)
    wq16 = ct.sb("wq16", [128, 1024], BF16)
    seg = ct.sb("seg", [128, 32], F32)
    sbv = ct.sb("sbv", [128, 32], F32)
    rsb = ct.sb("rsb", [128, 32], F32)
    wT = ct.sb("wT", [128, 8 * 128], BF16)
    tb16 = ct.sb("tb16", [128, 128], BF16)
    ev = ct.sb("ev", [128, 512], F32)
    evb = ct.sb("evb", [128, 512], BF16)
    mrow = ct.sb("mrow", [1, 16], F32)
    sxc = ct.sb("sxc", [128, 1], F32)
    rsxc = ct.sb("rsxc", [128, 1], F32)
    mcu = ct.sb("mcu", [128, 2], F32)
    mcg = ct.sb("mcg", [128, 2], F32)
    mr2 = ct.sb("mr2", [1, 128], F32)
    idb = ct.sb("idb", [128, 128], BF16)

    ct.begin()
    ones1f = load_ones(ct)
    ct._ones1f = ones1f
    m1bc = ct.cap(-1.0)
    p.dma("dg", lambda g: g.dma_start(out=idb[:, :], in_=identI),
          writes=["idb"])
    emit_global_scale(ct, p, mrow, MXI, 1.0 / 32767.0, sxc[:, 0:1],
                      rsxc[:, 0:1], P[0] if False else P[3], "mx")

    for i in range(8):
        for cc in range(TH // 1024):
            p.dma("dg", lambda g, i=i, cc=cc: g.dma_start(
                out=qw[:, :], in_=xTh[i * 128:(i + 1) * 128,
                                      cc * 1024:(cc + 1) * 1024]),
                writes=["qw"])
            emit_round_chain(ct, p, qw[:, :], ["qw", "mxrc"],
                             q716[i][:, cc * 1024:(cc + 1) * 1024], f"q7_{i}",
                             rsxc[:, 0:1], qx1[:, :], qti[:, :], qrr[:, :],
                             qdd[:, :], "q_", m1bc=m1bc)

    jobs = (
        (wupI, wupqT_d, 24, 1024, "wu", None),
        (wgtI, wgqT_d, 24, 1024, "wg", None),
        (wdnI, None, 8, 3072, "wd", WDQT),
    )
    for W, WqT_d, nrt, ncols, nm, WqT_out in jobs:
        for rt in range(nrt):
            for cc in range(ncols // 1024):
                p.dma("dg", lambda g, W=W, rt=rt, cc=cc: g.dma_start(
                    out=qw[:, :], in_=W[rt * 128:(rt + 1) * 128,
                                        cc * 1024:(cc + 1) * 1024]),
                    writes=["qw"])
                p.op("vector", lambda v: v.reduce_max(
                    seg[:, :], qw.rearrange("p (b k) -> p b k", k=32),
                    axis=AX.X, apply_absolute_value=True),
                    reads=["qw"], writes=["seg"])
                p.op("vector", lambda v: v.tensor_scalar_mul(
                    sbv[:, :], seg[:, :], 1.0 / 7.0),
                    reads=["seg"], writes=["sbv"])
                p.op("vector", lambda v: v.tensor_scalar_max(
                    sbv[:, :], sbv[:, :], 1e-12),
                    reads=["sbv"], writes=["sbv"])
                p.op("vector", lambda v: v.reciprocal(rsb[:, :], sbv[:, :]),
                     reads=["sbv"], writes=["rsb"])
                rsb3 = bcast_inner(rsb[:, :], 32)
                sb3 = bcast_inner(sbv[:, :], 32)
                emit_round_chain(
                    ct, p, qw.rearrange("p (b k) -> p b k", k=32),
                    ["qw", "rsb"],
                    wq16.rearrange("p (b k) -> p b k", k=32), "wq16",
                    rsb3, qx1.rearrange("p (b k) -> p b k", k=32),
                    qti.rearrange("p (b k) -> p b k", k=32),
                    qrr.rearrange("p (b k) -> p b k", k=32),
                    qdd.rearrange("p (b k) -> p b k", k=32),
                    "w_", rs_is_bcast3=True, sb_ap=sb3)
                ct.p.readers.setdefault("sbv", []).append(
                    ct.p.last_w.get("wq16"))
                for tc in range(8):
                    p.op("tensor", lambda t, tc=tc: t.transpose(
                        psT[:, :], wq16[:, tc * 128:(tc + 1) * 128], idb[:, :]),
                        reads=["wq16", "idb"], writes=["psT"])
                    p.op("vector", lambda v: v.tensor_copy(
                        tb16[:, :], psT[:, :]), reads=["psT"], writes=["tb16"])
                    dst = WqT_d if WqT_d is not None else WqT_out
                    p.dma("dh", lambda s, dst=dst, rt=rt, cc=cc, tc=tc:
                          s.dma_start(
                              out=dst[cc * 1024 + tc * 128:
                                      cc * 1024 + (tc + 1) * 128,
                                      rt * 128:(rt + 1) * 128],
                              in_=tb16[:, :]),
                          reads=["tb16"], writes=[nm + "qT"])

    for W_d, OUTT, mc, mk in ((wupqT_d, UP, mcu, "wuqT"),
                              (wgqT_d, GT, mcg, "wgqT")):
        for ot in range(24):
            p.dma("dg", lambda g, W_d=W_d, ot=ot: g.dma_start(
                out=wT.rearrange("p (i c) -> p i c", c=128),
                in_=W_d[:, ot * 128:(ot + 1) * 128]
                    .rearrange("(i p) c -> p i c", p=128)),
                reads=[mk], writes=["wT"])
            for tch in range(TH // 512):
                sl = slice(tch * 512, (tch + 1) * 512)
                for i in range(8):
                    p.op("tensor", lambda t, i=i, sl=sl: t.matmul(
                        P[0][:, :], wT[:, i * 128:(i + 1) * 128],
                        q716[i][:, sl], start=(i == 0), stop=(i == 7)),
                        reads=["wT", f"q7_{i}"], writes=["P0"])
                p.op("vector", lambda v: v.tensor_scalar(
                    evb[:, :], P[0][:, :], sxc[:, 0:1], None, OP.mult),
                    reads=["P0", "mxsc"], writes=["evb"])
                p.dma("dh", lambda s, OUTT=OUTT, ot=ot, sl=sl: s.dma_start(
                    out=OUTT[ot * 128:(ot + 1) * 128, sl], in_=evb[:, :]),
                    reads=["evb"], writes=["UPo"])
                p.op("vector", lambda v: v.reduce_max(
                    mcu[:, 1:2] if mc is mcu else mcg[:, 1:2], evb[:, :],
                    axis=AX.X, apply_absolute_value=True),
                    reads=["evb"], writes=["mcx1"])
                first = (ot == 0 and tch == 0)
                tgt = mcu if mc is mcu else mcg
                key = "mcu0" if mc is mcu else "mcg0"
                if first:
                    p.op("vector", lambda v, tgt=tgt: v.tensor_copy(
                        tgt[:, 0:1], tgt[:, 1:2]),
                        reads=["mcx1"], writes=[key])
                else:
                    p.op("vector", lambda v, tgt=tgt: v.tensor_max(
                        tgt[:, 0:1], tgt[:, 0:1], tgt[:, 1:2]),
                        reads=[key, "mcx1"], writes=[key])
    for tgt, key, OUTM, row in ((mcu, "mcu0", MU, 0), (mcg, "mcg0", MG, 1)):
        p.dma("dg", lambda g, tgt=tgt, row=row: g.dma_start(
            out=mx_d[row:row + 1, :], in_=tgt[:, 0:1]),
            reads=[key], writes=["mx_d8"])
        p.dma("dg", lambda g, row=row: g.dma_start(
            out=mr2[:, :], in_=mx_d[row:row + 1, :]),
            reads=["mx_d8"], writes=["mr2"])
        p.op("vector", lambda v: v.reduce_max(mr2[:, 0:1], mr2[:, :], axis=AX.X),
             reads=["mr2"], writes=["mr2"])
        p.dma("dh", lambda s, OUTM=OUTM: s.dma_start(out=OUTM, in_=mr2[:, 0:1]),
              reads=["mr2"], writes=["MUo"])

    return ct.done()


def a8_inputs(x7, mx8, I):
    ident = np.eye(128, dtype=np.float32)
    maps = []
    for c in range(NCORES):
        b, j = c // 2, c % 2
        sl = slice(j * TH, (j + 1) * TH)
        maps.append({
            "xTh": np.ascontiguousarray(x7[b].T[:, sl]),
            "MXall": mx8,
            "wup": I["L3_ffn_w_up"], "wgt": I["L3_ffn_w_gate"],
            "wdn": I["L3_ffn_w_down"], "identI": ident,
        })
    return maps




def build_A9():
    ct = Ctx("A9")
    p = ct.p
    UPI = ct.din("UP", [3072, TH], BF16)
    GTI = ct.din("GT", [3072, TH], BF16)
    MUI = ct.din("MUall", [1, 8])
    MGI = ct.din("MGall", [1, 8])
    UPQ = ct.dout("UPQ", [3072, TH], BF16)
    G2 = ct.dout("G2", [3072, TH])
    MG2 = ct.dout("MG2", [1, 1])
    mx_d = ct.nc.dram_tensor("mx_d9", [1, 128], F32).ap()

    P = [ct.ps(f"P{i}", [128, 512]) for i in range(2)]
    qw = ct.sb("qw", [128, 1024], F32)
    qx1 = ct.sb("qx1", [128, 1024], F32)
    qti = ct.sb("qti", [128, 1024], I32)
    qrr = ct.sb("qrr", [128, 1024], F32)
    qdd = ct.sb("qdd", [128, 1024], F32)
    gq = ct.sb("gq", [128, 1024], F32)
    sg = ct.sb("sg", [128, 1024], F32)
    g2 = ct.sb("g2", [128, 1024], F32)
    evb = ct.sb("evb", [128, 1024], BF16)
    mrow = ct.sb("mrow", [1, 16], F32)
    sxc = ct.sb("sxc", [128, 4], F32)
    mcol = ct.sb("mcol", [128, 2], F32)
    mr2 = ct.sb("mr2", [1, 128], F32)

    ct.begin()
    ones1f = load_ones(ct)
    ct._ones1f = ones1f
    m1bc = ct.cap(-1.0)
    emit_global_scale(ct, p, mrow, MUI, 1.0 / 32767.0, sxc[:, 0:1],
                      sxc[:, 1:2], P[1], "mu")
    emit_global_scale(ct, p, mrow, MGI, 1.0 / 32767.0, sxc[:, 2:3],
                      sxc[:, 3:4], P[1], "mg")

    for rt in range(24):
        for cc in range(TH // 1024):
            csl = slice(cc * 1024, (cc + 1) * 1024)
            rs = slice(rt * 128, (rt + 1) * 128)
            p.dma("dg", lambda g, rs=rs, csl=csl: g.dma_start(
                out=qw[:, :], in_=UPI[rs, csl]), writes=["qw"])
            emit_round_chain(ct, p, qw[:, :], ["qw", "murc"], evb[:, :],
                             "evb", sxc[:, 1:2], qx1[:, :], qti[:, :],
                             qrr[:, :], qdd[:, :], "q_", m1bc=m1bc)
            p.op("vector", lambda v: v.tensor_scalar(
                evb[:, :], qrr[:, :], sxc[:, 0:1], None, OP.mult),
                reads=["q_rr", "musc"], writes=["evb"])
            p.dma("dh", lambda s, rs=rs, csl=csl: s.dma_start(
                out=UPQ[rs, csl], in_=evb[:, :]),
                reads=["evb"], writes=["UPQo"])
            p.dma("dg", lambda g, rs=rs, csl=csl: g.dma_start(
                out=qw[:, :], in_=GTI[rs, csl]), writes=["qw"])
            emit_round_chain(ct, p, qw[:, :], ["qw", "mgrc"], gq[:, :],
                             "gq", sxc[:, 3:4], qx1[:, :], qti[:, :],
                             qrr[:, :], qdd[:, :], "q_",
                             post_scale=sxc[:, 2:3], m1bc=m1bc)
            p.op("scalar", lambda sc: sc.activation(sg[:, :], gq[:, :],
                                                    AF.Sigmoid),
                 reads=["gq"], writes=["sg"])
            p.op("vector", lambda v: v.tensor_scalar_mul(
                qx1[:, :], sg[:, :], 65536.0),
                reads=["sg"], writes=["q_x1"])
            p.op("vector", lambda v: v.tensor_scalar_add(
                qx1[:, :], qx1[:, :], -1.0),
                reads=["q_x1"], writes=["q_x1"])
            p.op("vector", lambda v: v.tensor_copy(qti[:, :], qx1[:, :]),
                 reads=["q_x1"], writes=["q_ti"])
            p.op("vector", lambda v: v.tensor_copy(qrr[:, :], qti[:, :]),
                 reads=["q_ti"], writes=["q_rr"])
            p.op("vector", lambda v: v.tensor_sub(qdd[:, :], qx1[:, :], qrr[:, :]),
                 reads=["q_x1", "q_rr"], writes=["q_dd"])
            p.op("vector", lambda v: v.scalar_tensor_tensor(
                qrr[:, :], qdd[:, :], 0.5, qrr[:, :], OP.is_ge, OP.add),
                reads=["q_dd", "q_rr"], writes=["q_rr"])
            p.op("vector", lambda v: v.scalar_tensor_tensor(
                qrr[:, :], qdd[:, :], -0.5, qrr[:, :], OP.is_ge, OP.add),
                reads=["q_dd", "q_rr"], writes=["q_rr"])
            p.op("vector", lambda v: v.tensor_scalar_min(
                qrr[:, :], qrr[:, :], 65535.0),
                reads=["q_rr"], writes=["q_rr"])
            p.op("vector", lambda v: v.tensor_mul(g2[:, :], gq[:, :], qrr[:, :]),
                 reads=["gq", "q_rr"], writes=["g2"])
            p.op("vector", lambda v: v.tensor_scalar_mul(
                g2[:, :], g2[:, :], 1.0 / 65536.0),
                reads=["g2"], writes=["g2"])
            p.dma("dh", lambda s, rs=rs, csl=csl: s.dma_start(
                out=G2[rs, csl], in_=g2[:, :]),
                reads=["g2"], writes=["G2o"])
            p.op("vector", lambda v: v.reduce_max(
                mcol[:, 1:2], g2[:, :], axis=AX.X, apply_absolute_value=True),
                reads=["g2"], writes=["mcx"])
            if rt == 0 and cc == 0:
                p.op("vector", lambda v: v.tensor_copy(
                    mcol[:, 0:1], mcol[:, 1:2]), reads=["mcx"], writes=["mc0"])
            else:
                p.op("vector", lambda v: v.tensor_max(
                    mcol[:, 0:1], mcol[:, 0:1], mcol[:, 1:2]),
                    reads=["mc0", "mcx"], writes=["mc0"])
    p.dma("dg", lambda g: g.dma_start(out=mx_d, in_=mcol[:, 0:1]),
          reads=["mc0"], writes=["mx_d9"])
    p.dma("dg", lambda g: g.dma_start(out=mr2[:, :], in_=mx_d),
          reads=["mx_d9"], writes=["mr2"])
    p.op("vector", lambda v: v.reduce_max(mr2[:, 0:1], mr2[:, :], axis=AX.X),
         reads=["mr2"], writes=["mr2"])
    p.dma("dh", lambda s: s.dma_start(out=MG2, in_=mr2[:, 0:1]),
          reads=["mr2"], writes=["MG2o"])
    return ct.done()



def build_A10():
    ct = Ctx("A10")
    p = ct.p
    G2I = ct.din("G2", [3072, TH])
    UPQI = ct.din("UPQ", [3072, TH], BF16)
    MG2I = ct.din("MG2all", [1, 8])
    OPRE = ct.dout("OPRE", [3072, TH])
    MO = ct.dout("MO", [1, 1])
    mx_d = ct.nc.dram_tensor("mx_d10", [1, 128], F32).ap()

    P = [ct.ps(f"P{i}", [128, 512]) for i in range(2)]
    qw = ct.sb("qw", [128, 1024], F32)
    up = ct.sb("up", [128, 1024], F32)
    qx1 = ct.sb("qx1", [128, 1024], F32)
    qti = ct.sb("qti", [128, 1024], I32)
    qrr = ct.sb("qrr", [128, 1024], F32)
    qdd = ct.sb("qdd", [128, 1024], F32)
    o_ = ct.sb("o_", [128, 1024], F32)
    mrow = ct.sb("mrow", [1, 16], F32)
    sxc = ct.sb("sxc", [128, 2], F32)
    mcol = ct.sb("mcol", [128, 2], F32)
    mr2 = ct.sb("mr2", [1, 128], F32)

    ct.begin()
    ones1f = load_ones(ct)
    ct._ones1f = ones1f
    m1bc = ct.cap(-1.0)
    emit_global_scale(ct, p, mrow, MG2I, 1.0 / 32767.0, sxc[:, 0:1],
                      sxc[:, 1:2], P[1], "m2")
    for rt in range(24):
        for cc in range(TH // 1024):
            csl = slice(cc * 1024, (cc + 1) * 1024)
            rs = slice(rt * 128, (rt + 1) * 128)
            p.dma("dg", lambda g, rs=rs, csl=csl: g.dma_start(
                out=qw[:, :], in_=G2I[rs, csl]), writes=["qw"])
            p.dma("dg", lambda g, rs=rs, csl=csl: g.dma_start(
                out=up[:, :], in_=UPQI[rs, csl]), writes=["up"])
            emit_round_chain(ct, p, qw[:, :], ["qw", "m2rc"], qrr[:, :],
                             "q_rr", sxc[:, 1:2], qx1[:, :], qti[:, :],
                             qrr[:, :], qdd[:, :], "q_", m1bc=m1bc)
            p.op("vector", lambda v: v.tensor_scalar(
                o_[:, :], qrr[:, :], sxc[:, 0:1], None, OP.mult),
                reads=["q_rr", "m2sc"], writes=["o_"])
            p.op("vector", lambda v: v.tensor_mul(o_[:, :], o_[:, :], up[:, :]),
                 reads=["o_", "up"], writes=["o_"])
            p.dma("dh", lambda s, rs=rs, csl=csl: s.dma_start(
                out=OPRE[rs, csl], in_=o_[:, :]),
                reads=["o_"], writes=["OPREo"])
            p.op("vector", lambda v: v.reduce_max(
                mcol[:, 1:2], o_[:, :], axis=AX.X, apply_absolute_value=True),
                reads=["o_"], writes=["mcx"])
            if rt == 0 and cc == 0:
                p.op("vector", lambda v: v.tensor_copy(
                    mcol[:, 0:1], mcol[:, 1:2]), reads=["mcx"], writes=["mc0"])
            else:
                p.op("vector", lambda v: v.tensor_max(
                    mcol[:, 0:1], mcol[:, 0:1], mcol[:, 1:2]),
                    reads=["mc0", "mcx"], writes=["mc0"])
    p.dma("dg", lambda g: g.dma_start(out=mx_d, in_=mcol[:, 0:1]),
          reads=["mc0"], writes=["mx_d10"])
    p.dma("dg", lambda g: g.dma_start(out=mr2[:, :], in_=mx_d),
          reads=["mx_d10"], writes=["mr2"])
    p.op("vector", lambda v: v.reduce_max(mr2[:, 0:1], mr2[:, :], axis=AX.X),
         reads=["mr2"], writes=["mr2"])
    p.dma("dh", lambda s: s.dma_start(out=MO, in_=mr2[:, 0:1]),
          reads=["mr2"], writes=["MOo"])
    return ct.done()



def build_A11():
    ct = Ctx("A11")
    p = ct.p
    OPREI = ct.din("OPRE", [3072, TH])
    MOI = ct.din("MOall", [1, 8])
    WDQTI = ct.din("WDQT", [3072, D], BF16)
    xTh = ct.din("xTh", [D, TH])
    X8 = ct.dout("X8", [D, TH])

    P = [ct.ps(f"P{i}", [128, 512]) for i in range(8)]
    xf = [ct.sb(f"xf{i}", [128, TH], F32) for i in range(8)]
    oq16 = [ct.sb(f"oq{i}", [128, TH], BF16) for i in range(24)]
    qw = ct.sb("qw", [128, 1024], F32)
    qx1 = ct.sb("qx1", [128, 1024], F32)
    qti = ct.sb("qti", [128, 1024], I32)
    qrr = ct.sb("qrr", [128, 1024], F32)
    qdd = ct.sb("qdd", [128, 1024], F32)
    wT = ct.sb("wT", [128, 24 * 128], BF16)
    ev = ct.sb("ev", [128, 512], F32)
    mrow = ct.sb("mrow", [1, 16], F32)
    sxc = ct.sb("sxc", [128, 2], F32)

    ct.begin()
    ones1f = load_ones(ct)
    ct._ones1f = ones1f
    m1bc = ct.cap(-1.0)
    lns = make_ln_scratch2(ct, P)
    emit_global_scale(ct, p, mrow, MOI, 1.0 / 32767.0, sxc[:, 0:1],
                      sxc[:, 1:2], P[4], "mo")
    for i in range(8):
        p.dma("dg", lambda g, i=i: g.dma_start(
            out=xf[i][:, :], in_=xTh[i * 128:(i + 1) * 128, :]),
            writes=[f"xf{i}"])
    for rt in range(24):
        for cc in range(TH // 1024):
            csl = slice(cc * 1024, (cc + 1) * 1024)
            p.dma("dg", lambda g, rt=rt, csl=csl: g.dma_start(
                out=qw[:, :], in_=OPREI[rt * 128:(rt + 1) * 128, csl]),
                writes=["qw"])
            emit_round_chain(ct, p, qw[:, :], ["qw", "morc"],
                             oq16[rt][:, csl], f"oq{rt}",
                             sxc[:, 1:2], qx1[:, :], qti[:, :], qrr[:, :],
                             qdd[:, :], "q_", m1bc=m1bc)
    for do in range(8):
        p.dma("dg", lambda g, do=do: g.dma_start(
            out=wT.rearrange("p (i c) -> p i c", c=128),
            in_=WDQTI[:, do * 128:(do + 1) * 128]
                .rearrange("(i p) c -> p i c", p=128)),
            writes=["wT"])
        for tch in range(TH // 512):
            sl = slice(tch * 512, (tch + 1) * 512)
            for i in range(24):
                p.op("tensor", lambda t, i=i, sl=sl: t.matmul(
                    P[4][:, :], wT[:, i * 128:(i + 1) * 128],
                    oq16[i][:, sl], start=(i == 0), stop=(i == 23)),
                    reads=["wT", f"oq{i}"], writes=["P4"])
            p.op("vector", lambda v: v.tensor_scalar(
                ev[:, :], P[4][:, :], sxc[:, 0:1], None, OP.mult),
                reads=["P4", "mosc"], writes=["ev"])
            p.op("vector", lambda v, do=do, sl=sl: v.tensor_add(
                xf[do][:, sl], xf[do][:, sl], ev[:, :]),
                reads=[f"xf{do}", "ev"], writes=[f"xf{do}"])
    xkeys = [f"xf{i}" for i in range(8)]
    emit_ln2(ct, xf, xkeys, lns, TH)
    for i in range(8):
        p.dma("dh", lambda s, i=i: s.dma_start(
            out=X8[i * 128:(i + 1) * 128, :], in_=xf[i][:, :]),
            reads=[f"xf{i}"], writes=[f"X8_{i}"])
    return ct.done()


def kernel(**inputs):
    I = {k: np.asarray(v) for k, v in inputs.items()}
    x = I["L0_attn_x"].astype(np.float32, copy=False)

    def run(nc, maps):
        return run_bass_kernel_spmd(nc, maps, core_ids=list(range(NCORES))).results

    r = run(build_A0(), a0_inputs(x, I["L0_attn_wq"], I["L0_attn_wk"],
                                  I["L0_attn_wv"]))
    ao = np.stack([np.concatenate([r[2 * b]["AO"], r[2 * b + 1]["AO"]], 0).T
                   for b in range(B)])

    r = run(build_A1(), a1_inputs(x, ao, I["L0_attn_wo"], I["L0_ffn_Wg"],
                                  I["L0_ffn_W1"], I["L0_ffn_W2"]))
    x2 = np.stack([np.concatenate([r[2 * b]["X2"].T, r[2 * b + 1]["X2"].T], 0)
                   for b in range(B)])

    r = run(build_A2(), a2_inputs(x2, I["L1_attn_W_attn"], I["L1_attn_b_attn"]))
    ao1 = np.stack([np.concatenate([r[2 * b]["AO1"], r[2 * b + 1]["AO1"]], 0).T
                    for b in range(B)])

    r = run(build_A3(), a3_inputs(x2, ao1, I))
    x4 = np.stack([np.concatenate([r[2 * b]["X4"].T, r[2 * b + 1]["X4"].T], 0)
                   for b in range(B)])

    r = run(build_A4(), a4_inputs(x4, I))
    delF = np.zeros((B, 2048, T), np.float32)
    xcF = np.zeros((B, 2048, T), np.float32)
    zzF = np.zeros((B, 2048, T), np.float32)
    bcF = np.zeros((B, 32, T), np.float32)
    for c in range(NCORES):
        b, j = c // 2, c % 2
        sl = slice(j * TH, (j + 1) * TH)
        delF[b][:, sl] = r[c]["DEL"]
        xcF[b][:, sl] = r[c]["XC"]
        zzF[b][:, sl] = r[c]["ZZ"]
        bcF[b][:, sl] = r[c]["BC"]

    r = run(build_A4b(), a4b_inputs(delF, xcF, zzF, bcF, I))
    opF = np.zeros((B, 2, 1024, T), np.float32)
    PT = np.zeros((D, D), np.float32)
    QT = np.zeros((D, D), np.float32)
    for c in range(NCORES):
        b, dh = c // 2, c % 2
        opF[b, dh] = r[c]["OPp"]
        PT[:, c * 128:(c + 1) * 128] = r[c]["PTs"]
        QT[:, c * 128:(c + 1) * 128] = r[c]["QTs"]

    r = run(build_A5(), a5_inputs(x4, opF, PT, QT, I))
    x6 = np.stack([np.concatenate([r[2 * b]["X6"].T, r[2 * b + 1]["X6"].T], 0)
                   for b in range(B)])
    del3F = np.zeros((B, D, T), np.float32)
    bc3F = np.zeros((B, 32, T), np.float32)
    for c in range(NCORES):
        b, j = c // 2, c % 2
        sl = slice(j * TH, (j + 1) * TH)
        del3F[b][:, sl] = r[c]["DEL3"]
        bc3F[b][:, sl] = r[c]["BC3"]

    r = run(build_A6(), a6_inputs(del3F, x6, bc3F, I))
    ssmF = np.zeros((B, D, T), np.float32)
    for c in range(NCORES):
        b, dh = c // 2, c % 2
        ssmF[b][dh * 512:(dh + 1) * 512] = r[c]["SSM"]

    r = run(build_A7(), a7_inputs(x6, ssmF, I))
    x7 = np.stack([np.concatenate([r[2 * b]["X7"].T, r[2 * b + 1]["X7"].T], 0)
                   for b in range(B)])
    mx8 = np.array([[r[c]["MX"][0, 0] for c in range(NCORES)]], np.float32)

    r8 = run(build_A8(), a8_inputs(x7, mx8, I))
    mu = np.array([[r8[c]["MU"][0, 0] for c in range(NCORES)]], np.float32)
    mg = np.array([[r8[c]["MG"][0, 0] for c in range(NCORES)]], np.float32)
    maps9 = [{"UP": r8[c]["UP"], "GT": r8[c]["GT"], "MUall": mu, "MGall": mg}
             for c in range(NCORES)]
    r9 = run(build_A9(), maps9)
    mg2 = np.array([[r9[c]["MG2"][0, 0] for c in range(NCORES)]], np.float32)
    maps10 = [{"G2": r9[c]["G2"], "UPQ": r9[c]["UPQ"], "MG2all": mg2}
              for c in range(NCORES)]
    r10 = run(build_A10(), maps10)
    mo = np.array([[r10[c]["MO"][0, 0] for c in range(NCORES)]], np.float32)
    maps11 = []
    for c in range(NCORES):
        b, j = c // 2, c % 2
        sl = slice(j * TH, (j + 1) * TH)
        maps11.append({"OPRE": r10[c]["OPRE"], "MOall": mo,
                       "WDQT": r8[c]["WDQT"],
                       "xTh": np.ascontiguousarray(x7[b].T[:, sl])})
    r11 = run(build_A11(), maps11)
    out = np.stack([np.concatenate([r11[2 * b]["X8"].T, r11[2 * b + 1]["X8"].T],
                                   0) for b in range(B)])
    return np.ascontiguousarray(out.astype(np.float32))
